# version 25
# baseline (speedup 1.0000x reference)
import contextlib
import numpy as np
import concourse.bass as bass
import concourse.mybir as mybir

F32 = mybir.dt.float32
BF16 = mybir.dt.bfloat16
AF = mybir.ActivationFunctionType
ALU = mybir.AluOpType
AX = mybir.AxisListType

D = 1024
EPS = 1e-5


class Sem:
    __slots__ = ("h", "name", "n")

    def __init__(self, handle, name):
        self.h = handle
        self.name = name
        self.n = 0


class Builder:
    ENGS = ("sync", "vector", "scalar", "tensor", "gpsimd")

    def __init__(self, nc):
        self.nc = nc
        self.q = {e: [] for e in self.ENGS}
        self.stack = contextlib.ExitStack()
        self.n_inst = 0
        self.n_waits = 0
        self.auto = {e: self.sem("au_" + e[:2]) for e in self.ENGS}
        self.writer = {}
        self.readers = {}
        self.know = {e: {} for e in self.ENGS}
        self.scopes = []
        self.pool_sems = []

    def sem(self, name):
        h = self.stack.enter_context(self.nc.semaphore(name))
        return Sem(h, name)

    def sbuf(self, name, shape, dtype):
        st = self.scopes[-1] if self.scopes else self.stack
        return st.enter_context(self.nc.sbuf_tensor(name, shape, dtype))

    def psum(self, name, shape, dtype):
        st = self.scopes[-1] if self.scopes else self.stack
        return st.enter_context(self.nc.psum_tensor(name, shape, dtype))

    def open_scope(self):
        self.scopes.append(contextlib.ExitStack())

    def close_scope(self):
        self.full_barrier()
        self.scopes.pop().close()

    def full_barrier(self):
        deps = [(s, s.n) for s in self.auto.values() if s.n > 0]
        deps += [(s, s.n) for s in self.pool_sems if s.n > 0]
        for e in self.ENGS:
            self.emit(e, None, wait=tuple(deps), track=False)

    def _edges(self, eng, rd, wr, suppress_self_pe=False):
        need = {}

        def add(dep):
            if dep is None:
                return
            s, v, peng = dep
            if suppress_self_pe and peng == eng == "tensor":
                return
            if self.know[eng].get(s, 0) >= v:
                return
            if need.get(s, 0) < v:
                need[s] = v
        for k in rd:
            add(self.writer.get(k))
        for k in wr:
            add(self.writer.get(k))
            for dep in self.readers.get(k, ()):
                add(dep)
        return need

    def emit(self, eng, fn=None, rd=(), wr=(), wait=(), track=True,
             suppress_self_pe=False, inck=1, sem=None):
        need = self._edges(eng, rd, wr, suppress_self_pe)
        for s, v in wait:
            if self.know[eng].get(s, 0) < v and need.get(s, 0) < v:
                need[s] = v
        waits = tuple(need.items())
        for s, v in waits:
            self.know[eng][s] = v
        tok = None
        if track and fn is not None:
            s = sem if sem is not None else self.auto[eng]
            s.n += inck
            tok = (s, s.n)
        self.n_inst += 1
        self.n_waits += len(waits)
        target = tok

        def play(e):
            for s_, v_ in waits:
                e.wait_ge(s_.h, v_)
            if fn is not None:
                ins = fn(e)
                if target is not None:
                    ins.then_inc(target[0].h, inck)
        self.q[eng].append(play)
        if tok is not None:
            dep = (tok[0], tok[1], eng)
            for k in wr:
                self.writer[k] = dep
                self.readers[k] = []
            for k in rd:
                self.readers.setdefault(k, []).append(dep)
        return tok

    def v(self, fn=None, **kw):
        return self.emit("vector", fn, **kw)

    def s(self, fn=None, **kw):
        return self.emit("scalar", fn, **kw)

    def t(self, fn=None, **kw):
        return self.emit("tensor", fn, **kw)

    def g(self, fn=None, **kw):
        return self.emit("gpsimd", fn, **kw)

    def y(self, fn=None, **kw):
        return self.emit("sync", fn, **kw)

    def barrier_wait(self, eng, deps):
        self.emit(eng, None, wait=deps, track=False)

    def warmup(self, bank_ap, bank_key, lhsT, rhs, lhs_key, rhs_key, n=20):
        for i in range(n):
            self.emit("tensor",
                      (lambda t, i=i: t.matmul(bank_ap, lhsT, rhs,
                                               start=True, stop=True)),
                      rd=(lhs_key, rhs_key),
                      wr=((bank_key,) if i in (0, n - 1) else ()),
                      track=(i == n - 1), suppress_self_pe=True)

    track_all_mm = True

    def acc_mm(self, bank_key, fn, rdk, first, last, pending):
        if first:
            pending.clear()
        for k in rdk:
            if k not in pending:
                pending.append(k)
        return self.emit("tensor", fn,
                         rd=(tuple(pending) if last else rdk),
                         wr=((bank_key,) if (first or last) else ()),
                         track=(last or self.track_all_mm),
                         suppress_self_pe=True)

    def mm_group(self, bank_key, items, sem=None):
        n = len(items)
        allrd = []
        seen = set()
        for _, rdk in items:
            for k in rdk:
                if k not in seen:
                    seen.add(k)
                    allrd.append(k)
        for j, (fn, rdk) in enumerate(items):
            self.emit("tensor", fn,
                      rd=(rdk if j < n - 1 else tuple(allrd)),
                      wr=((bank_key,) if j in (0, n - 1) else ()),
                      track=(j == n - 1 or self.track_all_mm),
                      suppress_self_pe=True, sem=sem)

    def finalize(self):
        while self.scopes:
            self.scopes.pop().close()
        nc = self.nc
        with nc.Block() as block:
            @block.sync
            def _(e):
                for f in self.q["sync"]:
                    f(e)

            @block.vector
            def _(e):
                for f in self.q["vector"]:
                    f(e)

            @block.scalar
            def _(e):
                for f in self.q["scalar"]:
                    f(e)

            @block.tensor
            def _(e):
                for f in self.q["tensor"]:
                    f(e)

            @block.gpsimd
            def _(e):
                for f in self.q["gpsimd"]:
                    f(e)
        self.stack.close()
        return nc


class DmaPool:

    def __init__(self, b, name, n, eng="sync"):
        self.b = b
        self.eng = eng
        self.sems = [b.sem(f"{name}{i}") for i in range(n)]
        b.pool_sems.extend(self.sems)
        self.i = 0

    def dma(self, out, in_, rd=(), wr=(), eng=None):
        b = self.b
        s = self.sems[self.i]
        self.i = (self.i + 1) % len(self.sems)
        return b.emit(eng or self.eng,
                      (lambda e, out=out, in_=in_: e.dma_start(out=out,
                                                               in_=in_)),
                      rd=rd, wr=wr, wait=((s, s.n),), inck=16, sem=s)

    def drain(self):
        return tuple((s, s.n) for s in self.sems)



def emit_identity(b, ident, key="ident"):
    b.g(lambda g: g.memset(ident[:], 0.0), wr=(key,))
    b.g(lambda g: g.affine_select(out=ident[:], in_=ident[:],
                                  compare_op=ALU.not_equal, fill=1.0, base=0,
                                  pattern=[[-1, 128]], channel_multiplier=1),
        rd=(key,), wr=(key,))


def emit_weightT_load_seq(b, pool, loads, wstage, wstage_key="wstage"):
    ns = wstage[:].shape[1]
    stage_key = "sbuf:" + wstage[:].tensor.name
    idx = 0
    for wt_dram, wbf, key in loads:
        src = wt_dram.rearrange("(a p) c -> p a c", p=128)
        for a in range(8):
            sl = idx % ns
            idx += 1
            pool.dma(wstage[:, sl:sl + 1, :], src[:, a:a + 1, :],
                     wr=((stage_key, sl),))
            b.v(lambda v, a=a, sl=sl, wbf=wbf: v.tensor_copy(
                wbf[:, a:a + 1, :], wstage[:, sl:sl + 1, :]),
                rd=((stage_key, sl),), wr=((key, a // 2),))


def emit_ln_slab(b, tiles, tile_keys, stats, scr, n=D, pref=""):
    nt = len(tiles)
    nh = n // 2
    kst = pref + "stats"
    K = pref + "scr"

    def fld(i):
        return scr[:, i * nt:(i + 1) * nt]

    for i, xap in enumerate(tiles):
        b.v(lambda v, xap=xap, i=i: v.bn_stats(stats[:, 12 * i:12 * i + 6],
                                               xap[:, 0:nh]),
            rd=(tile_keys[i],), wr=((kst, i, 0),))
        b.v(lambda v, xap=xap, i=i: v.bn_stats(stats[:, 12 * i + 6:12 * i + 12],
                                               xap[:, nh:n]),
            rd=(tile_keys[i],), wr=((kst, i, 1),))
    allst = tuple((kst, i, j) for i in range(nt) for j in range(2))
    st = lambda j: stats[:, j:12 * nt:12]
    m, ve, t1, t2, q, sq = (fld(i) for i in range(6))
    b.v(lambda v: v.tensor_add(t1, st(1), st(4)), rd=allst, wr=((K, "t1"),))
    b.v(lambda v: v.tensor_add(t2, st(7), st(10)), rd=allst, wr=((K, "t2"),))
    b.v(lambda v: v.tensor_add(t1, t1, t2), rd=((K, "t1"), (K, "t2")),
        wr=((K, "t1"),))
    b.v(lambda v: v.tensor_scalar_mul(m, t1, 0.25), rd=((K, "t1"),),
        wr=((K, "m"),))
    b.v(lambda v: v.tensor_mul(q, st(1), st(1)), rd=allst, wr=((K, "q"),))
    b.v(lambda v: v.tensor_mul(t2, st(4), st(4)), rd=allst, wr=((K, "t2"),))
    b.v(lambda v: v.tensor_add(q, q, t2), rd=((K, "q"), (K, "t2")),
        wr=((K, "q"),))
    b.v(lambda v: v.tensor_mul(t2, st(7), st(7)), rd=allst, wr=((K, "t2"),))
    b.v(lambda v: v.tensor_add(q, q, t2), rd=((K, "q"), (K, "t2")),
        wr=((K, "q"),))
    b.v(lambda v: v.tensor_mul(t2, st(10), st(10)), rd=allst, wr=((K, "t2"),))
    b.v(lambda v: v.tensor_add(q, q, t2), rd=((K, "q"), (K, "t2")),
        wr=((K, "q"),))
    b.v(lambda v: v.tensor_add(t1, st(2), st(5)), rd=allst, wr=((K, "t1"),))
    b.v(lambda v: v.tensor_add(t2, st(8), st(11)), rd=allst, wr=((K, "t2"),))
    b.v(lambda v: v.tensor_add(t1, t1, t2), rd=((K, "t1"), (K, "t2")),
        wr=((K, "t1"),))
    b.v(lambda v: v.tensor_scalar(ve, q, 0.25, None, ALU.mult),
        rd=((K, "q"),), wr=((K, "ve"),))
    b.v(lambda v: v.scalar_tensor_tensor(t2, t1, 1.0 / n, ve, ALU.mult,
                                         ALU.add),
        rd=((K, "t1"), (K, "ve")), wr=((K, "t2"),))
    b.v(lambda v: v.tensor_mul(t1, m, m), rd=((K, "m"),), wr=((K, "t1"),))
    b.v(lambda v: v.tensor_sub(ve, t2, t1), rd=((K, "t1"), (K, "t2")),
        wr=((K, "ve"),))
    b.v(lambda v: v.tensor_scalar(ve, ve, EPS, None, ALU.add),
        rd=((K, "ve"),), wr=((K, "ve"),))
    b.s(lambda s: s.activation(sq, ve, AF.Sqrt), rd=((K, "ve"),),
        wr=((K, "sq"),))
    rstd = q
    nb = t2
    b.v(lambda v: v.reciprocal(rstd, sq), rd=((K, "sq"),), wr=((K, "q"),))
    b.v(lambda v: v.scalar_tensor_tensor(nb, m, -1.0, rstd, ALU.mult,
                                         ALU.mult),
        rd=((K, "m"), (K, "q")), wr=((K, "t2"),))
    return rstd, nb, ((K, "q"), (K, "t2"))




def emit_lnT_seq(b, P, s, xin_rows, xt, ybf, yT, stats, scr, pT, ident,
                 ident_key, KXT, KYB, KYT, pin, affine=None, row0=None,
                 double=False, nt=4, ytk=None):
    base = s * 16 if row0 is None else row0
    nsl = 16 // nt
    if ytk is not None:
        assert affine is None
        allk = tuple(dict.fromkeys(ytk(sl) for sl in range(nsl)))
        b.v(lambda v: v.memset(yT[:, :, 0:1], 0.0), wr=allk)
    else:
        b.v(lambda v: v.memset(yT[:, :, 0:1], 0.0), wr=(KYT,))

    def loads(sl):
        pr = sl % 2
        for i in range(nt):
            row = base + sl * nt + i
            pin.dma(xt[:, pr * nt + i, :], xin_rows[row],
                    wr=(KXT(pr * nt + i),))

    ybf_par = ybf[:].shape[1] >= 2 * nt
    st_par = len(stats[:].shape) == 3

    loads(0)
    for sl in range(nsl):
        if sl + 1 < nsl:
            loads(sl + 1)
        pr = sl % 2
        yo = pr * nt if ybf_par else 0
        xs = lambda i, pr=pr: xt[:, pr * nt + i, :]
        stp = stats[:, pr] if st_par else stats
        scp = scr[:, pr] if st_par else scr
        prefp = P + f"s{pr}" if st_par else P
        rstd, nb, rnkeys = emit_ln_slab(
            b, [xs(i) for i in range(nt)],
            [KXT(pr * nt + i) for i in range(nt)], stp, scp, pref=prefp)
        for i in range(nt):
            b.s(lambda sx, i=i, rstd=rstd, nb=nb, xs=xs, yo=yo: sx.activation(
                ybf[:, yo + i, :], xs(i), AF.Identity,
                bias=nb[:, i:i + 1], scale=rstd[:, i:i + 1]),
                rd=(KXT(pr * nt + i),) + rnkeys, wr=(KYB(yo + i),))
        if double:
            rstd, nb, rnkeys = emit_ln_slab(
                b, [ybf[:, yo + i, :] for i in range(nt)],
                [KYB(yo + i) for i in range(nt)], stp, scp, pref=prefp)
            for i in range(nt):
                b.s(lambda sx, i=i, rstd=rstd, nb=nb, yo=yo: sx.activation(
                    ybf[:, yo + i, :], ybf[:, yo + i, :], AF.Identity,
                    bias=nb[:, i:i + 1], scale=rstd[:, i:i + 1]),
                    rd=(KYB(yo + i),) + rnkeys, wr=(KYB(yo + i),))
        for cb in range(8):
            bank = cb % 2
            items = []
            for i in range(nt):
                items.append((
                    (lambda t, cb=cb, i=i, bank=bank, yo=yo: t.transpose(
                        pT[bank][:, i, :],
                        ybf[:, yo + i, cb * 128:cb * 128 + 128], ident[:])),
                    (KYB(yo + i), ident_key)))
            b.mm_group((P + "pT", bank), items)
            b.s(lambda sx, cb=cb, bank=bank, sl=sl, nt=nt: sx.copy(
                yT[:, cb, 1 + sl * nt * 128: 1 + (sl + 1) * nt * 128],
                pT[bank][:, 0:nt, :].rearrange("p a q -> p (a q)")),
                rd=((P + "pT", bank),),
                wr=(ytk(sl),) if ytk is not None else (KYT,))
    if affine is not None:
        w8, b8, wkey, bkey = affine
        for cb in range(8):
            b.v(lambda v, cb=cb: v.tensor_scalar(
                yT[:, cb, 1:2049], yT[:, cb, 1:2049],
                w8[:, cb:cb + 1], b8[:, cb:cb + 1], ALU.mult, ALU.add),
                rd=(KYT, wkey, bkey), wr=(KYT,))


def emit_channel_ffn(b, x_in, x_out, wrT, wkT, wvT, mu_r8, mu_k8, pref=""):
    P = pref
    wr_bf = b.sbuf(P + "wr", [128, 8, 1024], BF16)
    wk_bf = b.sbuf(P + "wk", [128, 8, 1024], BF16)
    wv_bf = b.sbuf(P + "wv", [128, 8, 1024], BF16)
    wstage = b.sbuf(P + "wstage", [128, 2, 1024], F32)
    mur = b.sbuf(P + "smur", [128, 8], F32)
    muk = b.sbuf(P + "smuk", [128, 8], F32)
    ident = b.sbuf(P + "ident", [128, 128], BF16)
    yT = b.sbuf(P + "yT", [128, 8, 2049], BF16)
    ybf = b.sbuf(P + "ybf", [128, 4, 1024], BF16)
    xt = b.sbuf(P + "xt", [128, 8, 1024], F32)
    stats = b.sbuf(P + "stats", [128, 2, 48], F32)
    scr = b.sbuf(P + "scr", [128, 2, 24], F32)
    diffb = b.sbuf(P + "diff", [128, 512], BF16)
    mixr = b.sbuf(P + "mixr", [128, 8, 512], BF16)
    mixk = b.sbuf(P + "mixk", [128, 8, 512], BF16)
    actb = b.sbuf(P + "act", [128, 8, 512], BF16)
    srb = b.sbuf(P + "sr", [128, 4, 1024], F32)
    outf = b.sbuf(P + "outf", [128, 2, 4, 1024], F32)
    relsc = b.sbuf(P + "relsc", [128, 512], F32)
    pT = [b.psum(P + f"pT{i}", [128, 8, 128], BF16) for i in range(2)]
    pR = [b.psum(P + f"pR{i}", [128, 512], F32) for i in range(2)]
    pK = [b.psum(P + f"pK{i}", [128, 512], F32) for i in range(2)]
    pV = [b.psum(P + f"pV{i}", [128, 512], F32) for i in range(2)]

    pin = DmaPool(b, P + "pi", 8)
    pout = DmaPool(b, P + "po", 4, eng="gpsimd")

    emit_identity(b, ident, P + "ident")
    emit_weightT_load_seq(b, pin, [(wrT, wr_bf, P + "wr"),
                                   (wkT, wk_bf, P + "wk"),
                                   (wvT, wv_bf, P + "wv")],
                          wstage, P + "wstage")
    pin.dma(mur[:], mu_r8, wr=(P + "mur",))
    pin.dma(muk[:], mu_k8, wr=(P + "muk",))

    xin_rows = x_in.rearrange("(r p) c -> r p c", p=128)
    xout_rows = x_out.rearrange("(r p) c -> r p c", p=128)

    KXT = lambda i: (P + "xt", i)
    KYB = lambda i: (P + "ybf", i)
    KYT = P + "yT"
    KMIXR = lambda cb: (P + "mixr", cb)
    KMIXK = lambda cb: (P + "mixk", cb)
    KACT = lambda dt: (P + "act", dt)
    KSR = lambda mt: (P + "sr", mt)
    KOUTF = lambda pr, i: (P + "outf", pr, i)
    KLNB = lambda i: (P + "lnb", i)

    for s in range(2):
        b.v(lambda v: v.memset(yT[:, :, 0:1], 0.0),
            wr=tuple((KYT, k) for k in range(4)))
        def loads_pa(sl, s=s):
            pr = sl % 2
            for i in range(4):
                row = s * 16 + sl * 4 + i
                pin.dma(xt[:, pr * 4 + i, :], xin_rows[row],
                        wr=(KXT(pr * 4 + i),))

        loads_pa(0)
        for sl in range(4):
            if sl + 1 < 4:
                loads_pa(sl + 1)
            pr = sl % 2
            rstd, nb, rnkeys = emit_ln_slab(
                b, [xt[:, pr * 4 + i, :] for i in range(4)],
                [KXT(pr * 4 + i) for i in range(4)], stats[:, pr],
                scr[:, pr], pref=P + f"s{pr}")
            for i in range(4):
                b.s(lambda sx, i=i, rstd=rstd, nb=nb, pr=pr: sx.activation(
                    ybf[:, i, :], xt[:, pr * 4 + i, :], AF.Identity,
                    bias=nb[:, i:i + 1], scale=rstd[:, i:i + 1]),
                    rd=(KXT(pr * 4 + i),) + rnkeys, wr=(KYB(i),))
            for cb in range(8):
                bank = cb % 2
                items = []
                for i in range(4):
                    items.append((
                        (lambda t, cb=cb, i=i, bank=bank: t.transpose(
                            pT[bank][:, i, :],
                            ybf[:, i, cb * 128:cb * 128 + 128], ident[:])),
                        (KYB(i), P + "ident")))
                b.mm_group((P + "pT", bank), items)
                b.s(lambda sx, cb=cb, bank=bank, sl=sl: sx.copy(
                    yT[:, cb, 1 + sl * 512: 1 + sl * 512 + 512],
                    pT[bank][:, 0:4, :].rearrange("p a q -> p (a q)")),
                    rd=((P + "pT", bank),), wr=((KYT, sl),))
        def front(ch):
            off = ch * 512
            pr = ch % 2
            ytks = ((KYT, ch),) + (((KYT, ch - 1),) if ch > 0 else ())
            for cb in range(8):
                b.v(lambda v, cb=cb, off=off: v.tensor_sub(
                    diffb[:], yT[:, cb, 1 + off:1 + off + 512],
                    yT[:, cb, off:off + 512]),
                    rd=ytks, wr=(P + "diff",))
                b.v(lambda v, cb=cb, off=off: v.scalar_tensor_tensor(
                    mixr[:, cb, :], diffb[:], mur[:, cb:cb + 1],
                    yT[:, cb, off:off + 512], ALU.mult, ALU.add),
                    rd=(P + "diff", P + "mur") + ytks, wr=(KMIXR(cb),))
                b.v(lambda v, cb=cb, off=off: v.scalar_tensor_tensor(
                    mixk[:, cb, :], diffb[:], muk[:, cb:cb + 1],
                    yT[:, cb, off:off + 512], ALU.mult, ALU.add),
                    rd=(P + "diff", P + "muk") + ytks, wr=(KMIXK(cb),))
            for mt in range(4):
                for nh in range(2):
                    bank = (mt * 2 + nh) % 2
                    items = [
                        ((lambda t, mt=mt, nh=nh, ct=ct, bank=bank: t.matmul(
                            pR[bank][:],
                            mixr[:, ct, mt * 128:mt * 128 + 128],
                            wr_bf[:, ct, nh * 512:nh * 512 + 512],
                            start=(ct == 0), stop=(ct == 7))),
                         (KMIXR(ct), (P + "wr", ct // 2)))
                        for ct in range(8)]
                    b.mm_group((P + "pR", bank), items)
                    b.s(lambda sx, mt=mt, nh=nh, bank=bank: sx.activation(
                        srb[:, mt, nh * 512:nh * 512 + 512], pR[bank][:],
                        AF.Sigmoid),
                        rd=((P + "pR", bank),), wr=(KSR(mt),))
            for dt in range(8):
                bank = dt % 2
                items = [
                    ((lambda t, dt=dt, ct=ct, bank=bank: t.matmul(
                        pK[bank][:],
                        wk_bf[:, ct, dt * 128:dt * 128 + 128],
                        mixk[:, ct, :],
                        start=(ct == 0), stop=(ct == 7))),
                     (KMIXK(ct), (P + "wk", ct // 2)))
                    for ct in range(8)]
                b.mm_group((P + "pK", bank), items)
                b.v(lambda v, bank=bank: v.tensor_scalar_max(
                    relsc[:], pK[bank][:], 0.0),
                    rd=((P + "pK", bank),), wr=(P + "relsc",))
                b.v(lambda v, dt=dt: v.tensor_mul(actb[:, dt, :], relsc[:],
                                                  relsc[:]),
                    rd=(P + "relsc",), wr=(KACT(dt),))
            for mt in range(4):
                for nh in range(2):
                    bank = (mt * 2 + nh) % 2
                    items = [
                        ((lambda t, mt=mt, nh=nh, dt=dt, bank=bank: t.matmul(
                            pV[bank][:],
                            actb[:, dt, mt * 128:mt * 128 + 128],
                            wv_bf[:, dt, nh * 512:nh * 512 + 512],
                            start=(dt == 0), stop=(dt == 7))),
                         (KACT(dt), (P + "wv", dt // 2)))
                        for dt in range(8)]
                    b.mm_group((P + "pV", bank), items)
                    b.v(lambda v, mt=mt, nh=nh, bank=bank, pr=pr:
                        v.tensor_mul(
                            outf[:, pr, mt, nh * 512:nh * 512 + 512],
                            srb[:, mt, nh * 512:nh * 512 + 512], pV[bank][:]),
                        rd=((P + "pV", bank), KSR(mt)), wr=(KOUTF(pr, mt),))
        def tail(ch):
            pr = ch % 2
            for i in range(4):
                row = s * 16 + ch * 4 + i
                pin.dma(xt[:, pr * 4 + i, :], xin_rows[row],
                        wr=(KXT(pr * 4 + i),))
            rstd2, nb2, rnkeys2 = emit_ln_slab(
                b, [outf[:, pr, i, :] for i in range(4)],
                [KOUTF(pr, i) for i in range(4)], stats[:, pr],
                scr[:, pr], pref=P + f"s{pr}")
            for i in range(4):
                b.s(lambda sx, i=i, rstd2=rstd2, nb2=nb2, pr=pr:
                    sx.activation(
                        outf[:, pr, i, :], outf[:, pr, i, :], AF.Identity,
                        bias=nb2[:, i:i + 1], scale=rstd2[:, i:i + 1]),
                    rd=(KOUTF(pr, i),) + rnkeys2, wr=(KOUTF(pr, i),))
                b.v(lambda v, i=i, pr=pr: v.tensor_add(
                    outf[:, pr, i, :], outf[:, pr, i, :], xt[:, pr * 4 + i, :]),
                    rd=(KOUTF(pr, i), KXT(pr * 4 + i)), wr=(KOUTF(pr, i),))
                row = s * 16 + ch * 4 + i
                pout.dma(xout_rows[row], outf[:, pr, i, :],
                         rd=(KOUTF(pr, i),))

        for ch in range(4):
            front(ch)
            if ch > 0:
                tail(ch - 1)
        tail(3)
    b.barrier_wait("sync", pout.drain())
    return pout


def emit_rwkv_mixer(b, x_in, mT_out, w_key, w_value, w_recept,
                    ln1w8, ln1b8, tmk8, tmv8, tmr8, td8, tf8, pref="A"):
    P = pref
    CH = 256
    NCH = 8
    wk_bf = b.sbuf(P + "wk", [128, 8, 1024], BF16)
    wv_bf = b.sbuf(P + "wv", [128, 8, 1024], BF16)
    wr_bf = b.sbuf(P + "wr", [128, 8, 1024], BF16)
    wstage = b.sbuf(P + "wstage", [128, 2, 1024], F32)
    prm = b.sbuf(P + "prm", [128, 8, 8], F32)
    eu8 = b.sbuf(P + "eu8", [128, 8], F32)
    dk8 = b.sbuf(P + "dk8", [128, 8], F32)
    ident = b.sbuf(P + "ident", [128, 128], BF16)
    yT = b.sbuf(P + "yT", [128, 8, 2049], BF16)
    ybf = b.sbuf(P + "ybf", [128, 4, 1024], BF16)
    xt = b.sbuf(P + "xt", [128, 4, 1024], F32)
    stats = b.sbuf(P + "stats", [128, 2, 48], F32)
    scr = b.sbuf(P + "scr", [128, 2, 24], F32)
    diffc = b.sbuf(P + "diffc", [128, 2, 8, CH], BF16)
    mixb = b.sbuf(P + "mix", [128, 2, 8, CH], BF16)
    ek = b.sbuf(P + "ek", [128, 2, 8, CH], F32)
    d1n = b.sbuf(P + "d1n", [128, 2, 8, CH], F32)
    srT = b.sbuf(P + "srT", [128, 2, 8, CH], BF16)
    alph = b.sbuf(P + "alph", [128, 8, CH + 1], F32)
    beth = b.sbuf(P + "beth", [128, 8, CH + 1], F32)
    dkt = b.sbuf(P + "dkt", [128, 8, CH], F32)
    numt = b.sbuf(P + "numt", [128, 4, CH], F32)
    dent = b.sbuf(P + "dent", [128, 4, CH], F32)
    mTb = b.sbuf(P + "mTb", [128, 4, CH], BF16)
    pT = [b.psum(P + f"pT{i}", [128, 8, 128], BF16) for i in range(2)]
    pM = [b.psum(P + f"pM{i}", [128, 512], F32) for i in range(4)]

    pin = DmaPool(b, P + "pi", 8)
    pout = DmaPool(b, P + "po", 4, eng="gpsimd")

    emit_identity(b, ident, P + "ident")
    emit_weightT_load_seq(b, pin, [(w_key, wk_bf, P + "wk"),
                                   (w_value, wv_bf, P + "wv"),
                                   (w_recept, wr_bf, P + "wr")],
                          wstage, P + "wstage")
    for j, prm_d in enumerate((ln1w8, ln1b8, tmk8, tmv8, tmr8, td8, tf8)):
        pin.dma(prm[:, j, :], prm_d, wr=((P + "prm", j),))
    b.s(lambda s: s.activation(eu8[:], prm[:, 6, :], AF.Exp),
        rd=((P + "prm", 6),), wr=(P + "eu8",))
    b.s(lambda s: s.activation(dk8[:], prm[:, 5, :], AF.Exp),
        rd=((P + "prm", 5),), wr=(P + "dk8",))
    b.v(lambda v: v.tensor_scalar_mul(dk8[:], dk8[:], -1.0),
        rd=(P + "dk8",), wr=(P + "dk8",))
    b.s(lambda s: s.activation(dk8[:], dk8[:], AF.Exp),
        rd=(P + "dk8",), wr=(P + "dk8",))
    b.v(lambda v: v.memset(dkt[:], 0.0),
        wr=tuple((P + "dkt", dt) for dt in range(8)))
    for dt in range(8):
        b.v(lambda v, dt=dt: v.tensor_scalar(
            dkt[:, dt, :], dkt[:, dt, :], 0.0, dk8[:, dt:dt + 1],
            ALU.mult, ALU.add),
            rd=(P + "dk8",), wr=((P + "dkt", dt),))

    xin_rows = x_in.rearrange("(r p) c -> r p c", p=128)
    mT_rows = mT_out.rearrange("(a p) t -> a p t", p=128)

    KXT = lambda i: (P + "xt", i)
    KYB = lambda i: (P + "ybf", i)
    KYT = P + "yT"
    KMIX = lambda pr, cb: (P + "mix", pr, cb)
    KEK = lambda pr, dt: (P + "ek", pr, dt)
    KD1 = lambda pr, dt: (P + "d1n", pr, dt)
    KAL = lambda dt: (P + "al", dt)
    KBE = lambda dt: (P + "be", dt)
    KSR = lambda pr, dt: (P + "srT", pr, dt)
    KM = lambda dt: (P + "mTb", dt % 2)

    wmats = {"k": (wk_bf, P + "wk"), "v": (wv_bf, P + "wv"),
             "r": (wr_bf, P + "wr")}

    def emit_front(s, ch):
        pr = ch % 2
        off = ch * CH
        for cb in range(8):
            b.v(lambda v, cb=cb, off=off, pr=pr: v.tensor_sub(
                diffc[:, pr, cb, :], yT[:, cb, 1 + off:1 + off + CH],
                yT[:, cb, off:off + CH]),
                rd=(KYT,), wr=((P + "diffc", pr, cb),))

        def mixes(prm_j, prm_key):
            for cb in range(8):
                b.v(lambda v, cb=cb, off=off, j=prm_j, pr=pr:
                    v.scalar_tensor_tensor(
                        mixb[:, pr, cb, :], diffc[:, pr, cb, :],
                        prm[:, j, cb:cb + 1],
                        yT[:, cb, off:off + CH], ALU.mult, ALU.add),
                    rd=((P + "diffc", pr, cb), KYT, prm_key),
                    wr=(KMIX(pr, cb),))

        def mm_into(dt, bank, wname):
            wbf, wkey = wmats[wname]
            items = [
                ((lambda t, dt=dt, ct=ct, bank=bank, wbf=wbf, pr=pr: t.matmul(
                    pM[bank][:, 0:CH],
                    wbf[:, ct, dt * 128:dt * 128 + 128],
                    mixb[:, pr, ct, :],
                    start=(ct == 0), stop=(ct == 7))),
                 (KMIX(pr, ct), (wkey, ct // 2)))
                for ct in range(8)]
            b.mm_group((P + "pM", bank), items)

        mixes(2, (P + "prm", 2))
        for dt in range(8):
            bank = dt % 2
            mm_into(dt, bank, "k")
            b.s(lambda sx, dt=dt, bank=bank, pr=pr: sx.activation(
                ek[:, pr, dt, :], pM[bank][:, 0:CH], AF.Exp),
                rd=((P + "pM", bank),), wr=(KEK(pr, dt),))
        mixes(3, (P + "prm", 3))
        for dt in range(8):
            bank = 2 + dt % 2
            mm_into(dt, bank, "v")
            b.v(lambda v, dt=dt, bank=bank, pr=pr: v.tensor_mul(
                d1n[:, pr, dt, :], ek[:, pr, dt, :], pM[bank][:, 0:CH]),
                rd=((P + "pM", bank), KEK(pr, dt)), wr=(KD1(pr, dt),))
        mixes(4, (P + "prm", 4))
        for dt in range(8):
            bank = dt % 2
            mm_into(dt, bank, "r")
            b.s(lambda sx, dt=dt, bank=bank, pr=pr: sx.activation(
                srT[:, pr, dt, :], pM[bank][:, 0:CH], AF.Sigmoid),
                rd=((P + "pM", bank),), wr=(KSR(pr, dt),))

    def emit_wkv(s, ch):
        pr = ch % 2
        gtok = s * 2048 + ch * CH
        KN = lambda dp: (P + "numt", dp)
        KDN = lambda dp: (P + "dent", dp)

        def front(dt):
            dp = dt % 4
            b.s(lambda sx, dt=dt, pr=pr, dp=dp: sx.activation(
                numt[:, dp, :], d1n[:, pr, dt, :], AF.Identity,
                scale=eu8[:, dt:dt + 1]),
                rd=(KD1(pr, dt), P + "eu8"), wr=(KN(dp),))
            b.s(lambda sx, dt=dt, pr=pr, dp=dp: sx.activation(
                dent[:, dp, :], ek[:, pr, dt, :], AF.Identity,
                scale=eu8[:, dt:dt + 1]),
                rd=(KEK(pr, dt), P + "eu8"), wr=(KDN(dp),))
            b.v(lambda v, dt=dt: v.tensor_copy(alph[:, dt, 0:1],
                                               alph[:, dt, CH:CH + 1]),
                rd=(KAL(dt),), wr=(KAL(dt),))
            b.v(lambda v, dt=dt: v.tensor_copy(beth[:, dt, 0:1],
                                               beth[:, dt, CH:CH + 1]),
                rd=(KBE(dt),), wr=(KBE(dt),))
            b.v(lambda v, dt=dt, pr=pr: v.scalar_tensor_tensor(
                d1n[:, pr, dt, 0:1], alph[:, dt, 0:1], dk8[:, dt:dt + 1],
                d1n[:, pr, dt, 0:1], ALU.mult, ALU.add),
                rd=(KAL(dt), P + "dk8", KD1(pr, dt)), wr=(KD1(pr, dt),))
            b.v(lambda v, dt=dt, pr=pr: v.scalar_tensor_tensor(
                ek[:, pr, dt, 0:1], beth[:, dt, 0:1], dk8[:, dt:dt + 1],
                ek[:, pr, dt, 0:1], ALU.mult, ALU.add),
                rd=(KBE(dt), P + "dk8", KEK(pr, dt)), wr=(KEK(pr, dt),))
            b.v(lambda v, dt=dt, pr=pr: v.tensor_tensor_scan(
                alph[:, dt, 1:CH + 1], dkt[:, dt, :], d1n[:, pr, dt, :], 0.0,
                ALU.mult, ALU.add),
                rd=((P + "dkt", dt), KD1(pr, dt)), wr=(KAL(dt),))
            b.v(lambda v, dt=dt, pr=pr: v.tensor_tensor_scan(
                beth[:, dt, 1:CH + 1], dkt[:, dt, :], ek[:, pr, dt, :], 0.0,
                ALU.mult, ALU.add),
                rd=((P + "dkt", dt), KEK(pr, dt)), wr=(KBE(dt),))
            b.v(lambda v, dt=dt, dp=dp: v.tensor_add(
                numt[:, dp, :], numt[:, dp, :], alph[:, dt, 0:CH]),
                rd=(KN(dp), KAL(dt)), wr=(KN(dp),))
            b.v(lambda v, dt=dt, dp=dp: v.tensor_add(
                dent[:, dp, :], dent[:, dp, :], beth[:, dt, 0:CH]),
                rd=(KDN(dp), KBE(dt)), wr=(KDN(dp),))

        def tail(dt):
            dp = dt % 4
            b.s(lambda sx, dp=dp: sx.activation(dent[:, dp, :],
                                                dent[:, dp, :], AF.Ln),
                rd=(KDN(dp),), wr=(KDN(dp),))
            b.s(lambda sx, dp=dp: sx.activation(dent[:, dp, :],
                                                dent[:, dp, :], AF.Exp,
                                                scale=-1.0),
                rd=(KDN(dp),), wr=(KDN(dp),))
            b.v(lambda v, dp=dp: v.tensor_mul(numt[:, dp, :], numt[:, dp, :],
                                              dent[:, dp, :]),
                rd=(KN(dp), KDN(dp)), wr=(KN(dp),))
            b.v(lambda v, dt=dt, pr=pr, dp=dp: v.tensor_mul(
                mTb[:, dt % 4, :], numt[:, dp, :], srT[:, pr, dt, :]),
                rd=(KN(dp), KSR(pr, dt)), wr=(KM(dt),))
            pout.dma(mT_rows[dt][:, gtok:gtok + CH], mTb[:, dt % 4, :],
                     rd=(KM(dt),))

        for dt in range(8):
            front(dt)
            if dt > 1:
                tail(dt - 2)
        tail(6)
        tail(7)

    for s in range(2):
        emit_lnT_seq(b, P, s, xin_rows, xt, ybf, yT, stats, scr, pT,
                     ident, P + "ident", KXT, KYB, KYT, pin,
                     affine=(prm[:, 0, :], prm[:, 1, :],
                             (P + "prm", 0), (P + "prm", 1)), double=True,
                     nt=2)
        b.v(lambda v: v.memset(alph[:, :, CH:CH + 1], 0.0),
            wr=tuple(KAL(dt) for dt in range(8)))
        b.v(lambda v: v.memset(beth[:, :, CH:CH + 1], 0.0),
            wr=tuple(KBE(dt) for dt in range(8)))
        for ch in range(NCH):
            emit_front(s, ch)
            if ch > 0:
                emit_wkv(s, ch - 1)
        emit_wkv(s, NCH - 1)
    b.barrier_wait("sync", pout.drain())
    return pout


def emit_wide_load(b, pool, dram, wbf, key, wstage, rows, cols):
    src = dram.rearrange("(a p) c -> p a c", p=128)
    na = rows // 128
    ncc = cols // 1024
    ns = wstage[:].shape[1]
    stage_key = "sbuf:" + wstage[:].tensor.name
    idx = 0
    for a in range(na):
        for cc in range(ncc):
            sl = idx % ns
            idx += 1
            pool.dma(wstage[:, sl:sl + 1, :],
                     src[:, a:a + 1, 1024 * cc:1024 * cc + 1024],
                     wr=((stage_key, sl),))
            b.v(lambda v, a=a, cc=cc, sl=sl, wbf=wbf: v.tensor_copy(
                wbf[:, a:a + 1, 1024 * cc:1024 * cc + 1024],
                wstage[:, sl:sl + 1, :]),
                rd=((stage_key, sl),), wr=((key, a // 2, cc),))


def wkeys(key, na=8, ncc=1, aw=2):
    return tuple((key, a2, cc) for a2 in range(na // aw) for cc in range(ncc))


def emit_rwkv_ffn(b, x_in, mT_in, x1_out, w_output, fwk_d, frcpt_d, fwv_d,
                  ln2w8, ln2b8, ftmk8, ftmr8, z_scr, actT_scr, srf_scr,
                  pref="F"):
    P = pref
    xin_rows = x_in.rearrange("(r p) c -> r p c", p=128)
    x1_rows = x1_out.rearrange("(r p) c -> r p c", p=128)
    z_rows = z_scr.rearrange("(r p) c -> r p c", p=128)
    srf_rows = srf_scr.rearrange("(r p) c -> r p c", p=128)
    mT_rows = mT_in.rearrange("(a p) t -> a p t", p=128)
    actT_rows = actT_scr.rearrange("(a p) t -> a p t", p=128)

    pin = DmaPool(b, P + "pi", 8)
    pout = DmaPool(b, P + "po", 4, eng="gpsimd")

    b.open_scope()
    wo_bf = b.sbuf(P + "wo", [128, 8, 1024], BF16)
    wstage = b.sbuf(P + "wstage", [128, 4, 1024], F32)
    mTc = b.sbuf(P + "mTc", [128, 2, 8, 512], BF16)
    xt = b.sbuf(P + "xt", [128, 8, 1024], F32)
    zt = b.sbuf(P + "zt", [128, 2, 4, 1024], F32)
    stats = b.sbuf(P + "stats", [128, 2, 48], F32)
    scr = b.sbuf(P + "scr", [128, 2, 24], F32)
    pA = [b.psum(P + f"pA{i}", [128, 512], F32) for i in range(2)]
    emit_wide_load(b, pin, w_output, wo_bf, P + "wo", wstage, 1024, 1024)
    KXT = lambda i: (P + "xt", i)
    KZT = lambda pr, i: (P + "zt", pr, i)
    KMTC = lambda pr: (P + "mTc", pr)

    def loads_a1(u):
        pr = u % 2
        gtok = u * 512
        pin.dma(mTc[:, pr], mT_rows[:, :, gtok:gtok + 512].rearrange(
            "a p t -> p a t"), wr=(KMTC(pr),))
        for i in range(4):
            pin.dma(xt[:, pr * 4 + i, :], xin_rows[u * 4 + i],
                    wr=(KXT(pr * 4 + i),))

    loads_a1(0)
    for u in range(8):
        if u + 1 < 8:
            loads_a1(u + 1)
        pr = u % 2
        rstd, nb, rnk = emit_ln_slab(
            b, [xt[:, pr * 4 + i, :] for i in range(4)],
            [KXT(pr * 4 + i) for i in range(4)], stats[:, pr],
            scr[:, pr], pref=P + f"s{pr}")
        for i in range(4):
            b.s(lambda sx, i=i, rstd=rstd, nb=nb, pr=pr: sx.activation(
                zt[:, pr, i, :], xt[:, pr * 4 + i, :], AF.Identity,
                bias=nb[:, i:i + 1], scale=rstd[:, i:i + 1]),
                rd=(KXT(pr * 4 + i),) + rnk, wr=(KZT(pr, i),))
        for mt in range(4):
            for nh in range(2):
                bank = (mt * 2 + nh) % 2
                items = [
                    ((lambda t, mt=mt, nh=nh, dt=dt, bank=bank, pr=pr:
                      t.matmul(
                          pA[bank][:],
                          mTc[:, pr, dt, mt * 128:mt * 128 + 128],
                          wo_bf[:, dt, nh * 512:nh * 512 + 512],
                          start=(dt == 0), stop=(dt == 7))),
                     (KMTC(pr),) + wkeys(P + "wo"))
                    for dt in range(8)]
                b.mm_group((P + "pA", bank), items)
                b.v(lambda v, mt=mt, nh=nh, bank=bank, pr=pr: v.tensor_add(
                    zt[:, pr, mt, nh * 512:nh * 512 + 512],
                    zt[:, pr, mt, nh * 512:nh * 512 + 512], pA[bank][:]),
                    rd=((P + "pA", bank), KZT(pr, mt)), wr=(KZT(pr, mt),))
        for i in range(4):
            pout.dma(z_rows[u * 4 + i], zt[:, pr, i, :], rd=(KZT(pr, i),))
    b.close_scope()
    b.open_scope()
    fwk_bf = b.sbuf(P + "fwk", [128, 8, 4096], BF16)
    fr_bf = b.sbuf(P + "frc", [128, 8, 1024], BF16)
    wstage2 = b.sbuf(P + "wst2", [128, 4, 1024], F32)
    prm = b.sbuf(P + "prm", [128, 4, 8], F32)
    ident = b.sbuf(P + "ident", [128, 128], BF16)
    yT = b.sbuf(P + "yT", [128, 8, 2049], BF16)
    ybf = b.sbuf(P + "ybf", [128, 4, 1024], BF16)
    xt2 = b.sbuf(P + "xt2", [128, 8, 1024], F32)
    stats2 = b.sbuf(P + "st2", [128, 2, 48], F32)
    scr2 = b.sbuf(P + "sc2", [128, 2, 24], F32)
    diffb = b.sbuf(P + "diff", [128, 512], BF16)
    mixb = b.sbuf(P + "mix", [128, 8, 512], BF16)
    actb = b.sbuf(P + "actb", [128, 2, 512], BF16)
    srfb = b.sbuf(P + "srfb", [128, 4, 1024], BF16)
    relsc = b.sbuf(P + "relsc", [128, 512], F32)
    pT = [b.psum(P + f"pT{i}", [128, 8, 128], BF16) for i in range(2)]
    pB = [b.psum(P + f"pB{i}", [128, 512], F32) for i in range(2)]
    emit_identity(b, ident, P + "ident")
    emit_wide_load(b, pin, fwk_d, fwk_bf, P + "fwk", wstage2, 1024, 4096)
    emit_wide_load(b, pin, frcpt_d, fr_bf, P + "frc", wstage2, 1024, 1024)
    for j, prm_d in enumerate((ln2w8, ln2b8, ftmk8, ftmr8)):
        pin.dma(prm[:, j, :], prm_d, wr=((P + "prm", j),))
    KXT2 = lambda i: (P + "xt2", i)
    KYB = lambda i: (P + "ybf", i)
    KYT = P + "yT"
    KMIX = lambda cb: (P + "mix", cb)
    KSRF = lambda i: (P + "srfb", i)
    for s in range(2):
        emit_lnT_seq(b, P, s, z_rows, xt2, ybf, yT, stats2, scr2, pT,
                     ident, P + "ident", KXT2, KYB, KYT, pin,
                     affine=(prm[:, 0, :], prm[:, 1, :],
                             (P + "prm", 0), (P + "prm", 1)))
        for ch in range(4):
            off = ch * 512
            gtok = s * 2048 + off
            for cb in range(8):
                b.v(lambda v, cb=cb, off=off: v.tensor_sub(
                    diffb[:], yT[:, cb, 1 + off:1 + off + 512],
                    yT[:, cb, off:off + 512]),
                    rd=(KYT,), wr=(P + "diff",))
                b.v(lambda v, cb=cb, off=off: v.scalar_tensor_tensor(
                    mixb[:, cb, :], diffb[:], prm[:, 2, cb:cb + 1],
                    yT[:, cb, off:off + 512], ALU.mult, ALU.add),
                    rd=(P + "diff", KYT, (P + "prm", 2)), wr=(KMIX(cb),))
            for ft in range(32):
                bank = ft % 2
                items = [
                    ((lambda t, ft=ft, ct=ct, bank=bank: t.matmul(
                        pB[bank][:],
                        fwk_bf[:, ct, ft * 128:ft * 128 + 128],
                        mixb[:, ct, :],
                        start=(ct == 0), stop=(ct == 7))),
                     (KMIX(ct), (P + "fwk", ct // 2, ft // 8)))
                    for ct in range(8)]
                b.mm_group((P + "pB", bank), items)
                b.v(lambda v, bank=bank: v.tensor_scalar_max(
                    relsc[:], pB[bank][:], 0.0),
                    rd=((P + "pB", bank),), wr=(P + "relsc",))
                b.v(lambda v, ft=ft: v.tensor_mul(
                    actb[:, ft % 2, :], relsc[:], relsc[:]),
                    rd=(P + "relsc",), wr=((P + "actb", ft % 2),))
                pout.dma(actT_rows[ft][:, gtok:gtok + 512],
                         actb[:, ft % 2, :], rd=((P + "actb", ft % 2),))
            for cb in range(8):
                b.v(lambda v, cb=cb, off=off: v.tensor_sub(
                    diffb[:], yT[:, cb, 1 + off:1 + off + 512],
                    yT[:, cb, off:off + 512]),
                    rd=(KYT,), wr=(P + "diff",))
                b.v(lambda v, cb=cb, off=off: v.scalar_tensor_tensor(
                    mixb[:, cb, :], diffb[:], prm[:, 3, cb:cb + 1],
                    yT[:, cb, off:off + 512], ALU.mult, ALU.add),
                    rd=(P + "diff", KYT, (P + "prm", 3)), wr=(KMIX(cb),))
            for mt in range(4):
                for nh in range(2):
                    bank = (mt * 2 + nh) % 2
                    items = [
                        ((lambda t, mt=mt, nh=nh, ct=ct, bank=bank: t.matmul(
                            pB[bank][:],
                            mixb[:, ct, mt * 128:mt * 128 + 128],
                            fr_bf[:, ct, nh * 512:nh * 512 + 512],
                            start=(ct == 0), stop=(ct == 7))),
                         (KMIX(ct), (P + "frc", ct // 2, 0)))
                        for ct in range(8)]
                    b.mm_group((P + "pB", bank), items)
                    b.s(lambda sx, mt=mt, nh=nh, bank=bank: sx.activation(
                        srfb[:, mt, nh * 512:nh * 512 + 512], pB[bank][:],
                        AF.Sigmoid),
                        rd=((P + "pB", bank),), wr=(KSRF(mt),))
            for i in range(4):
                row = s * 16 + ch * 4 + i
                pout.dma(srf_rows[row], srfb[:, i, :], rd=(KSRF(i),))
    b.close_scope()
    b.open_scope()
    fwv_bf = b.sbuf(P + "fwv", [128, 32, 1024], BF16)
    wstage3 = b.sbuf(P + "wst3", [128, 2, 1024], F32)
    actc = b.sbuf(P + "actc", [128, 2, 32, 512], BF16)
    zc = b.sbuf(P + "zc", [128, 2, 4, 1024], F32)
    srfc = b.sbuf(P + "srfc", [128, 2, 4, 1024], BF16)
    xc = b.sbuf(P + "xc", [128, 4, 1024], F32)
    gsc = b.sbuf(P + "gsc", [128, 512], F32)
    stats3 = b.sbuf(P + "st3", [128, 2, 48], F32)
    scr3 = b.sbuf(P + "sc3", [128, 2, 24], F32)
    pC = [b.psum(P + f"pC{i}", [128, 512], F32) for i in range(2)]
    emit_wide_load(b, pin, fwv_d, fwv_bf, P + "fwv", wstage3, 4096, 1024)
    KZC = lambda pr, i: (P + "zc", pr, i)

    def loads(u):
        s, ch = divmod(u, 4)
        pr = u % 2
        gtok = s * 2048 + ch * 512
        pin.dma(actc[:, pr], actT_rows[:, :, gtok:gtok + 512].rearrange(
            "a p t -> p a t"), wr=((P + "actc", pr),))
        for i in range(4):
            row = s * 16 + ch * 4 + i
            pin.dma(zc[:, pr, i, :], z_rows[row], wr=(KZC(pr, i),))
            pin.dma(srfc[:, pr, i, :], srf_rows[row],
                    wr=((P + "srfc", pr, i),))

    def compute(u):
        s, ch = divmod(u, 4)
        pr = u % 2
        for i in range(4):
            row = s * 16 + ch * 4 + i
            pin.dma(xc[:, i, :], xin_rows[row], wr=((P + "xc", i),))
        for mt in range(4):
            for nh in range(2):
                bank = (mt * 2 + nh) % 2
                items = [
                    ((lambda t, mt=mt, nh=nh, ft=ft, bank=bank, pr=pr:
                      t.matmul(
                          pC[bank][:],
                          actc[:, pr, ft, mt * 128:mt * 128 + 128],
                          fwv_bf[:, ft, nh * 512:nh * 512 + 512],
                          start=(ft == 0), stop=(ft == 31))),
                     ((P + "actc", pr), (P + "fwv", ft, 0)))
                    for ft in range(32)]
                b.mm_group((P + "pC", bank), items)
                b.v(lambda v, mt=mt, nh=nh, bank=bank, pr=pr: v.tensor_mul(
                    gsc[:], srfc[:, pr, mt, nh * 512:nh * 512 + 512],
                    pC[bank][:]),
                    rd=((P + "pC", bank), (P + "srfc", pr, mt)),
                    wr=(P + "gsc",))
                b.v(lambda v, mt=mt, nh=nh, pr=pr: v.tensor_add(
                    zc[:, pr, mt, nh * 512:nh * 512 + 512],
                    zc[:, pr, mt, nh * 512:nh * 512 + 512], gsc[:]),
                    rd=(P + "gsc", KZC(pr, mt)), wr=(KZC(pr, mt),))
        rstd3, nb3, rnk3 = emit_ln_slab(
            b, [zc[:, pr, i, :] for i in range(4)],
            [KZC(pr, i) for i in range(4)], stats3[:, pr], scr3[:, pr],
            pref=P + f"Bs{pr}")
        for i in range(4):
            b.s(lambda sx, i=i, rstd3=rstd3, nb3=nb3, pr=pr: sx.activation(
                zc[:, pr, i, :], zc[:, pr, i, :], AF.Identity,
                bias=nb3[:, i:i + 1], scale=rstd3[:, i:i + 1]),
                rd=(KZC(pr, i),) + rnk3, wr=(KZC(pr, i),))
            b.v(lambda v, i=i, pr=pr: v.tensor_add(
                zc[:, pr, i, :], zc[:, pr, i, :], xc[:, i, :]),
                rd=(KZC(pr, i), (P + "xc", i)), wr=(KZC(pr, i),))
            row = (s * 16 + ch * 4 + i)
            pout.dma(x1_rows[row], zc[:, pr, i, :], rd=(KZC(pr, i),))

    loads(0)
    for u in range(8):
        if u < 7:
            loads(u + 1)
        compute(u)
    b.close_scope()
    b.barrier_wait("sync", pout.drain())
    return pout


def emit_l1_attn(b, x_in, x_out, wqkv_d, wout_d, odil_scr, pref="L"):
    P = pref
    xin_rows = x_in.rearrange("(r p) c -> r p c", p=128)
    xout_rows = x_out.rearrange("(r p) c -> r p c", p=128)
    odil_v = odil_scr.rearrange("(s i g) d -> s g i d", s=2, g=4)
    wq_v = wqkv_d.rearrange("(a p) f -> p a f", p=128)

    pin = DmaPool(b, P + "pi", 8)
    pout = DmaPool(b, P + "po", 4, eng="gpsimd")
    ident = b.sbuf(P + "ident", [128, 128], BF16)
    emit_identity(b, ident, P + "ident")
    kT = b.sbuf(P + "kT", [128, 8, 2048], BF16)
    qT = b.sbuf(P + "qT", [128, 8, 2048], BF16)
    vaug = b.sbuf(P + "vaug", [128, 16, 1040], BF16)
    yT = b.sbuf(P + "yT", [128, 8, 2049], BF16)
    KKT = lambda dt: (P + "kT", dt)
    KQT = lambda dt: (P + "qT", dt)
    KVA = P + "vaug"
    KVT = lambda dt: (P + "vT", dt)
    KVD = P + "vdil"
    KYT = P + "yT"

    def _seq(s):
        S = P + str(s)
        b.open_scope()
        vdil = b.sbuf(S + "vdil", [128, 16, 1040], BF16)
        b.open_scope()
        vT = b.sbuf(S + "vT", [128, 8, 2048], BF16)
        b.open_scope()
        wstg = b.sbuf(S + "wstg", [128, 4, 128], F32)
        wblk = b.sbuf(S + "wblk", [128, 8, 128], BF16)
        ybf = b.sbuf(S + "ybf", [128, 1, 1024], BF16)
        xt = b.sbuf(S + "xt", [128, 2, 1024], F32)
        stats = b.sbuf(S + "st", [128, 2, 12], F32)
        scr = b.sbuf(S + "sc", [128, 2, 6], F32)
        pT = [b.psum(S + f"pT{i}", [128, 8, 128], BF16) for i in range(2)]
        pQ = [b.psum(S + f"pQ{i}", [128, 512], F32) for i in range(2)]
        KXT = lambda i: (P + "xt", i)
        KYB = lambda i: (P + "ybf", i)
        emit_lnT_seq(b, P, s, xin_rows, xt, ybf, yT, stats, scr, pT,
                     ident, P + "ident", KXT, KYB, KYT, pin, nt=1,
                     ytk=lambda sl: (P + "yT", sl // 4))
        KWB = P + "wblk"

        def load_wblk(col):
            for h in range(2):
                pin.dma(wstg[:], wq_v[:, h * 4:h * 4 + 4, col:col + 128],
                        wr=(P + "wstg",))
                b.v(lambda v, h=h: v.tensor_copy(
                    wblk[:, h * 4:h * 4 + 4, :], wstg[:]),
                    rd=(P + "wstg",), wr=(KWB,))

        for which, fofs, dst, dkey, scale in (
                ("k", 1024, kT, KKT, None), ("q", 0, qT, KQT, 0.125),
                ("v", 2048, vT, KVT, None)):
            for dt in range(8):
                load_wblk(fofs + dt * 128)
                for tc in range(4):
                    bank = tc % 2
                    items = [
                        ((lambda t, ct=ct, tc=tc, bank=bank: t.matmul(
                            pQ[bank][:],
                            wblk[:, ct, :],
                            yT[:, ct, 1 + tc * 512:1 + tc * 512 + 512],
                            start=(ct == 0), stop=(ct == 7))),
                         ((P + "yT", tc), KWB))
                        for ct in range(8)]
                    b.mm_group((P + "pQ", bank), items)
                    if scale is None:
                        b.s(lambda sx, dt=dt, tc=tc, bank=bank, dst=dst:
                            sx.copy(
                                dst[:, dt, tc * 512:tc * 512 + 512],
                                pQ[bank][:]),
                            rd=((P + "pQ", bank),), wr=((dkey(dt)),))
                    else:
                        b.s(lambda sx, dt=dt, tc=tc, bank=bank, dst=dst,
                            scale=scale: sx.activation(
                                dst[:, dt, tc * 512:tc * 512 + 512],
                                pQ[bank][:], AF.Identity, scale=scale),
                            rd=((P + "pQ", bank),), wr=((dkey(dt)),))
        b.v(lambda v: v.memset(vaug[:, :, 64:1040:65], 1.0), wr=(KVA,))
        for jt in range(16):
            for dtile in range(8):
                b.emit("tensor",
                       (lambda t, jt=jt, dtile=dtile: t.transpose(
                           pT[0][:, dtile, :],
                           vT[:, dtile, jt * 128:jt * 128 + 128], ident[:])),
                       rd=(KVT(dtile), P + "ident"),
                       wr=((P + "pT", 0),) if dtile in (0, 7) else (),
                       track=(dtile == 7), suppress_self_pe=True)
            b.s(lambda sx, jt=jt: sx.copy(
                vaug[:, jt, :].rearrange("p (h c) -> p h c",
                                         c=65)[:, :, 0:64],
                pT[0][:].rearrange("p a q -> p (a q)").rearrange(
                    "p (h c) -> p h c", c=64)),
                rd=((P + "pT", 0),), wr=(KVA,))
        b.close_scope()
        b.open_scope()
        pY = b.psum(S + "pY", [128, 128], BF16)
        b.v(lambda v: v.memset(vdil[:, :, 64:1040:65], 1.0), wr=(KVD,))
        for g in range(4):
            for jp in range(4):
                for dtile in range(8):
                    b.emit("tensor",
                           (lambda t, g=g, jp=jp, dtile=dtile: t.transpose(
                               pY[:], vT[:, dtile,
                                         g + 512 * jp:g + 512 * jp + 509:4],
                               ident[:])),
                           rd=(KVT(dtile), P + "ident"),
                           wr=((P + "pY",),), track=True,
                           suppress_self_pe=True)
                    b.s(lambda sx, g=g, jp=jp, dtile=dtile: sx.copy(
                        vdil[:, g * 4 + jp,
                             dtile * 130:dtile * 130 + 130].rearrange(
                            "p (h c) -> p h c", c=65)[:, :, 0:64],
                        pY[:].rearrange("p (h c) -> p h c", c=64)),
                        rd=((P + "pY",),), wr=(KVD,))
        b.close_scope()
        b.close_scope()
        b.open_scope()
        pscC = [b.psum(S + f"Cpsc{i}", [128, 2, 512], F32) for i in range(2)]
        poC = [b.psum(S + f"Cpo{i}", [128, 512], F32) for i in range(2)]
        pt2C = b.psum(S + "Cpt2", [128, 4, 66], BF16)
        ptbC = b.sbuf(S + "Cptb", [128, 2, 2, 512], BF16)
        obufC = b.sbuf(S + "Cobuf", [128, 2, 512], BF16)
        odil = b.sbuf(S + "odil", [128, 16, 1040], BF16)
        rscD = b.sbuf(S + "rscD", [128, 16, 16], F32)
        KPT = lambda i: (P + "ptb", i)
        KOB = lambda i: (P + "obuf", i)
        KOD = P + "odil"

        def emit_sc_dil(g, hp, jp):
            par = jp % 2
            for hb in range(2):
                b.t(lambda t, g=g, hp=hp, jp=jp, hb=hb, par=par: t.matmul(
                    pscC[par][:, hb, :],
                    kT[64 * hb:64 * hb + 64, hp,
                       g + 512 * jp:g + 512 * jp + 509:4],
                    qT[64 * hb:64 * hb + 64, hp, g::4],
                    start=True, stop=True,
                    tile_position=(64 * hb, 0)),
                    rd=(KKT(hp), KQT(hp)), wr=((P + "psc", par, hb),),
                    suppress_self_pe=True, track=True)
            b.s(lambda sx, par=par: sx.activation(
                ptbC[:, par].rearrange("p a q -> p (a q)"),
                pscC[par][:].rearrange("p a q -> p (a q)"), AF.Exp),
                rd=((P + "psc", par, 0), (P + "psc", par, 1)),
                wr=(KPT(par),))

        def emit_pv_dil(g, hp, jp):
            par = jp % 2
            for hb in range(2):
                h = hp * 2 + hb
                b.acc_mm((P + "po", hb),
                         (lambda t, g=g, jp=jp, h=h, hb=hb, par=par: t.matmul(
                             poC[hb][0:65, :],
                             vdil[:, g * 4 + jp, h * 65:h * 65 + 65],
                             ptbC[:, par, hb, :],
                             start=(jp == 0), stop=(jp == 3),
                             skip_group_check=True)),
                         (KVD, KPT(par)), jp == 0, jp == 3, pendC[hb])

        def tail_dil(g, hp):
            for hb in range(2):
                h = hp * 2 + hb
                b.v(lambda v, hb=hb: v.tensor_copy(
                    obufC[0:65, hb, :], poC[hb][0:65, :]),
                    rd=((P + "po", hb),), wr=(KOB(hb),))
            for hb in range(2):
                h = hp * 2 + hb
                items = [
                    ((lambda t, hb=hb, ib=ib: t.transpose(
                        pt2C[:, ib, 0:65], obufC[0:65, hb,
                                                 ib * 128:ib * 128 + 128],
                        ident[0:65, 0:65])),
                     (KOB(hb), P + "ident"))
                    for ib in range(4)]
                b.mm_group((P + "pt2",), items)
                for ib in range(4):
                    b.v(lambda v, g=g, ib=ib, h=h: v.tensor_copy(
                        odil[:, g * 4 + ib, h * 65:h * 65 + 65],
                        pt2C[:, ib, 0:65]),
                        rd=((P + "pt2",),), wr=(KOD,))

        b.warmup(pscC[0][:, 0, :], (P + "psc", 0, 0), ident[:],
                 kT[:, 0, 0:512], P + "ident", KKT(0))
        pendC = [[], []]
        units = [(g, hp) for g in range(4) for hp in range(8)]
        tail = None
        for g, hp in units:
            emit_sc_dil(g, hp, 0)
            for jp in range(4):
                if jp < 3:
                    emit_sc_dil(g, hp, jp + 1)
                if jp == 0 and tail is not None:
                    tail()
                    tail = None
                emit_pv_dil(g, hp, jp)
            tail = (lambda g=g, hp=hp: tail_dil(g, hp))
        tail()
        b.v(lambda v: v.reciprocal(rscD[:], odil[:, :, 64:1040:65]),
            rd=(KOD,), wr=(P + "rscD",))
        for sl in range(16):
            for h in range(16):
                b.v(lambda v, sl=sl, h=h: v.tensor_scalar(
                    odil[:, sl, h * 65:h * 65 + 64],
                    odil[:, sl, h * 65:h * 65 + 64],
                    rscD[:, sl, h:h + 1], 1.0 / 3.0,
                    ALU.mult, ALU.mult),
                    rd=(KOD, P + "rscD"), wr=(KOD,))
            pout.dma(odil_v[s, sl // 4,
                            (sl % 4) * 128:(sl % 4) * 128 + 128,
                            :].rearrange("i (h c) -> i h c", c=64),
                     odil[:, sl, :].rearrange("p (h c) -> p h c",
                                              c=65)[:, :, 0:64],
                     rd=(KOD,), wr=(P + "odildram",))
        b.close_scope()
        b.close_scope()
        b.open_scope()
        psc = [b.psum(S + f"Dpsc{i}", [128, 2, 512], F32) for i in range(2)]
        po = [b.psum(S + f"Dpo{i}", [128, 512], F32) for i in range(2)]
        pBF = b.psum(S + "DpBF", [128, 8, 128], BF16)
        pW = [psc[1][:, 0, :], psc[1][:, 1, :]]
        ptb = b.sbuf(S + "Dptb", [128, 2, 2, 512], BF16)
        obuf = b.sbuf(S + "Dobuf", [128, 2, 512], BF16)
        oglb = b.sbuf(S + "oglb", [128, 4, 1040], BF16)
        oloc = b.sbuf(S + "oloc", [128, 4, 1040], BF16)
        odc = b.sbuf(S + "odc", [128, 4, 1024], BF16)
        oTb = b.sbuf(S + "oTb", [128, 8, 512], BF16)
        wout = b.sbuf(S + "wout", [128, 8, 1024], BF16)
        wstage4 = b.sbuf(S + "wst4", [128, 1, 1024], F32)
        xt4 = b.sbuf(S + "xt4", [128, 1024], F32)
        att4 = b.sbuf(S + "att4", [128, 4, 1024], F32)
        stats4 = b.sbuf(S + "st4", [128, 48], F32)
        scr4 = b.sbuf(S + "sc4", [128, 24], F32)
        rsG = b.sbuf(S + "rsG", [128, 4, 16], F32)
        rsL = b.sbuf(S + "rsL", [128, 4, 16], F32)
        fen = b.sbuf(S + "Dfen", [128, 1], F32)
        emit_wide_load(b, pin, wout_d, wout, P + "wout", wstage4, 1024, 1024)
        KPT = lambda i: (P + "ptb", i)
        KOB = lambda i: (P + "obuf", i)
        KOG = P + "oglb"
        KOL = P + "oloc"

        def emit_sc_glb(hp, it, jt):
            par = jt % 2
            for hb in range(2):
                b.t(lambda t, hp=hp, jt=jt, it=it, hb=hb, par=par: t.matmul(
                    psc[par][:, hb, :],
                    kT[64 * hb:64 * hb + 64, hp, jt * 128:jt * 128 + 128],
                    qT[64 * hb:64 * hb + 64, hp, it * 512:it * 512 + 512],
                    start=True, stop=True,
                    tile_position=(64 * hb, 0)),
                    rd=(KKT(hp), KQT(hp)), wr=((P + "psc", par, hb),),
                    suppress_self_pe=True, track=True)
            b.s(lambda sx, par=par: sx.activation(
                ptb[:, par].rearrange("p a q -> p (a q)"),
                psc[par][:].rearrange("p a q -> p (a q)"), AF.Exp),
                rd=((P + "psc", par, 0), (P + "psc", par, 1)),
                wr=(KPT(par),))

        def emit_pv_glb(hp, jt):
            par = jt % 2
            for hb in range(2):
                h = hp * 2 + hb
                b.acc_mm((P + "po", hb),
                         (lambda t, jt=jt, h=h, hb=hb, par=par: t.matmul(
                             po[hb][0:65, :],
                             vaug[:, jt, h * 65:h * 65 + 65],
                             ptb[:, par, hb, :],
                             start=(jt == 0), stop=(jt == 15),
                             skip_group_check=True)),
                         (KVA, KPT(par)), jt == 0, jt == 15, pendG[hb])

        def tail_glb(hp):
            for hb in range(2):
                b.v(lambda v, hb=hb: v.tensor_copy(
                    obuf[0:65, hb, :], po[hb][0:65, :]),
                    rd=((P + "po", hb),), wr=(KOB(hb), KOB(2 + hb)))
            for hb in range(2):
                h = hp * 2 + hb
                items = [
                    ((lambda t, hb=hb, ib=ib: t.transpose(
                        pBF[:, ib, 0:65], obuf[0:65, hb,
                                               ib * 128:ib * 128 + 128],
                        ident[0:65, 0:65])),
                     (KOB(hb), P + "ident"))
                    for ib in range(4)]
                b.mm_group((P + "pBF",), items)
                for ib in range(4):
                    b.v(lambda v, ib=ib, h=h: v.tensor_copy(
                        oglb[:, ib, h * 65:h * 65 + 65], pBF[:, ib, 0:65]),
                        rd=((P + "pBF",),), wr=(KOG,))

        def emit_sc_loc(hp, it, ib):
            w = it * 4 + ib
            par = ib % 2
            for hb in range(2):
                b.t(lambda t, hp=hp, w=w, hb=hb, par=par: t.matmul(
                    psc[par][:, hb, 0:128],
                    kT[64 * hb:64 * hb + 64, hp, w * 128:w * 128 + 128],
                    qT[64 * hb:64 * hb + 64, hp, w * 128:w * 128 + 128],
                    start=True, stop=True,
                    tile_position=(64 * hb, 0)),
                    rd=(KKT(hp), KQT(hp)), wr=((P + "psc", par, hb),),
                    suppress_self_pe=True, track=True)
            b.s(lambda sx, par=par: sx.activation(
                ptb[:, par, :, 0:128], psc[par][:, :, 0:128], AF.Exp),
                rd=((P + "psc", par, 0), (P + "psc", par, 1)),
                wr=(KPT(par),))

        def tail_loc(hp, it, ib):
            par = ib % 2
            co = 256 * par
            for hb in range(2):
                h = hp * 2 + hb
                b.t(lambda t, w=it * 4 + ib, h=h, hb=hb, par=par, co=co:
                    t.matmul(
                        po[hb][0:65, co:co + 128],
                        vaug[:, w, h * 65:h * 65 + 65],
                        ptb[:, par, hb, 0:128],
                        start=True, stop=True),
                    rd=(KVA, KPT(par)),
                    wr=((P + "po", hb, par), (P + "po", hb)),
                    suppress_self_pe=True, track=True)
            for hb in range(2):
                b.v(lambda v, hb=hb, par=par, co=co: v.tensor_copy(
                    obuf[0:65, hb, co:co + 128], po[hb][0:65, co:co + 128]),
                    rd=((P + "po", hb, par),), wr=(KOB(par * 2 + hb),))
            for hb in range(2):
                h = hp * 2 + hb
                b.emit("tensor",
                       (lambda t, hb=hb, par=par, co=co: t.transpose(
                           pBF[:, 4 + hb, 0:65], obuf[0:65, hb, co:co + 128],
                           ident[0:65, 0:65])),
                       rd=(KOB(par * 2 + hb), P + "ident"),
                       wr=((P + "pBF",),),
                       track=True, suppress_self_pe=True)
                b.v(lambda v, ib=ib, h=h, hb=hb: v.tensor_copy(
                    oloc[:, ib, h * 65:h * 65 + 65], pBF[:, 4 + hb, 0:65]),
                    rd=((P + "pBF",),), wr=(KOL,))

        def it_tail(it):
                pin.dma(odc[:],
                        odil_scr.rearrange("(r p) c -> r p c", p=128)[
                            s * 16 + it * 4:s * 16 + it * 4 + 4].rearrange(
                            "r p c -> p r c"),
                        rd=(P + "odildram",), wr=(P + "odc",))
                for buf, key, rbuf, rkey in ((oglb, KOG, rsG, P + "rsG"),
                                             (oloc, KOL, rsL, P + "rsL")):
                    b.v(lambda v, buf=buf, rbuf=rbuf: v.reciprocal(
                        rbuf[:], buf[:, :, 64:1040:65]),
                        rd=(key,), wr=(rkey,))
                for ib in range(4):
                    for h in range(16):
                        b.v(lambda v, ib=ib, h=h: v.tensor_scalar(
                            oloc[:, ib, h * 65:h * 65 + 64],
                            oloc[:, ib, h * 65:h * 65 + 64],
                            rsL[:, ib, h:h + 1], 1.0 / 3.0,
                            ALU.mult, ALU.mult),
                            rd=(KOL, P + "rsL"), wr=(KOL,))
                        b.v(lambda v, ib=ib, h=h: v.tensor_scalar(
                            oglb[:, ib, h * 65:h * 65 + 64],
                            oglb[:, ib, h * 65:h * 65 + 64],
                            rsG[:, ib, h:h + 1], 1.0 / 3.0,
                            ALU.mult, ALU.mult),
                            rd=(KOG, P + "rsG"), wr=(KOG,))
                    b.v(lambda v, ib=ib: v.tensor_add(
                        odc[:, ib, :].rearrange("p (h c) -> p h c", c=64),
                        odc[:, ib, :].rearrange("p (h c) -> p h c", c=64),
                        oloc[:, ib, :].rearrange("p (h c) -> p h c",
                                                 c=65)[:, :, 0:64]),
                        rd=(P + "odc", KOL), wr=(P + "odc",))
                    b.v(lambda v, ib=ib: v.tensor_add(
                        odc[:, ib, :].rearrange("p (h c) -> p h c", c=64),
                        odc[:, ib, :].rearrange("p (h c) -> p h c", c=64),
                        oglb[:, ib, :].rearrange("p (h c) -> p h c",
                                                 c=65)[:, :, 0:64]),
                        rd=(P + "odc", KOG), wr=(P + "odc",))
                for ib in range(4):
                    items = [
                        ((lambda t, ib=ib, cb=cb: t.transpose(
                            pBF[:, cb, :], odc[:, ib, cb * 128:cb * 128 + 128],
                            ident[:])),
                         (P + "odc", P + "ident"))
                        for cb in range(8)]
                    b.mm_group((P + "pBF",), items)
                    for cb in range(8):
                        b.v(lambda v, ib=ib, cb=cb: v.tensor_copy(
                            oTb[:, cb, ib * 128:ib * 128 + 128], pBF[:, cb, :]),
                            rd=((P + "pBF",),), wr=((P + "oTb", cb),))
                for mt in range(4):
                    for nh in range(2):
                        bank = (mt * 2 + nh) % 2
                        items = [
                            ((lambda t, mt=mt, nh=nh, dt=dt, bank=bank: t.matmul(
                                pW[bank],
                                oTb[:, dt, mt * 128:mt * 128 + 128],
                                wout[:, dt, nh * 512:nh * 512 + 512],
                                start=(dt == 0), stop=(dt == 7))),
                             ((P + "oTb", dt),) + wkeys(P + "wout", aw=1))
                            for dt in range(8)]
                        b.mm_group((P + "psc", 1, bank), items)
                        b.v(lambda v, mt=mt, nh=nh, bank=bank: v.tensor_copy(
                            att4[:, mt, nh * 512:nh * 512 + 512], pW[bank]),
                            rd=((P + "psc", 1, bank),), wr=((P + "att4", mt),))
                rstd4, nb4, rnk4 = emit_ln_slab(
                    b, [att4[:, i, :] for i in range(4)],
                    [(P + "att4", i) for i in range(4)], stats4, scr4,
                    pref=P + "D")
                for i in range(4):
                    row = s * 16 + it * 4 + i
                    pin.dma(xt4[:], xin_rows[row], wr=(P + "xt4",))
                    b.s(lambda sx, i=i, rstd4=rstd4, nb4=nb4: sx.activation(
                        att4[:, i, :], att4[:, i, :], AF.Identity,
                        bias=nb4[:, i:i + 1], scale=rstd4[:, i:i + 1]),
                        rd=((P + "att4", i),) + rnk4, wr=((P + "att4", i),))
                    b.v(lambda v, i=i: v.tensor_add(
                        att4[:, i, :], att4[:, i, :], xt4[:]),
                        rd=((P + "att4", i), P + "xt4"), wr=((P + "att4", i),))
                    pout.dma(xout_rows[row], att4[:, i, :],
                             rd=((P + "att4", i),))

        it_pending = [None]

        b.warmup(psc[0][:, 0, :], (P + "psc", 0, 0), ident[:],
                 kT[:, 0, 0:512], P + "ident", KKT(0))
        for it in range(4):
            pendG = [[], []]
            gtail = None
            for hp in range(8):
                emit_sc_glb(hp, it, 0)
                for jt in range(16):
                    if jt < 15:
                        emit_sc_glb(hp, it, jt + 1)
                    if jt == 0 and gtail is not None:
                        gtail()
                        gtail = None
                    if hp == 0 and jt == 2 and it_pending[0] is not None:
                        it_pending[0]()
                        it_pending[0] = None
                    emit_pv_glb(hp, jt)
                gtail = (lambda hp=hp: tail_glb(hp))
            gtail()
            ltail = None
            for hp in range(8):
                for ib in range(4):
                    emit_sc_loc(hp, it, ib)
                    if ltail is not None:
                        ltail()
                        ltail = None
                    ltail = (lambda hp=hp, ib=ib: tail_loc(hp, it, ib))
            ltail()
            b.v(lambda v: v.memset(fen[:], 0.0),
                rd=((P + "po", 0), (P + "po", 1)), wr=(P + "Dfen",))
            it_pending[0] = (lambda it=it: it_tail(it))
        it_pending[0]()
        b.close_scope()

    for s in range(2):
        _seq(s)
    b.barrier_wait("sync", pout.drain())
    return pout


from concourse.bass_utils import run_bass_kernel_spmd


def _ap(nc, n, sh, dt=F32, kind="ExternalInput"):
    return nc.dram_tensor(n, sh, dt, kind=kind).ap()


def build_fused():
    nc = bass.Bass("TRN2", target_bir_lowering=False, debug=False)
    x_in = _ap(nc, "x_in", [4096, 1024])
    wk = _ap(nc, "wk", [1024, 1024])
    wv = _ap(nc, "wv", [1024, 1024])
    wr = _ap(nc, "wr", [1024, 1024])
    p1p = {n: _ap(nc, n, [128, 8]) for n in
           ("ln1w", "ln1b", "tmk", "tmv", "tmr", "td", "tf")}
    wo = _ap(nc, "wo", [1024, 1024])
    fwk = _ap(nc, "fwk", [1024, 4096])
    frc = _ap(nc, "frc", [1024, 1024])
    fwv = _ap(nc, "fwv", [4096, 1024])
    p2p = {n: _ap(nc, n, [128, 8]) for n in ("ln2w", "ln2b", "ftmk", "ftmr")}
    g_wrT = _ap(nc, "g_wrT", [1024, 1024])
    g_wkT = _ap(nc, "g_wkT", [1024, 1024])
    g_wvT = _ap(nc, "g_wvT", [1024, 1024])
    g_mur = _ap(nc, "g_mur", [128, 8])
    g_muk = _ap(nc, "g_muk", [128, 8])
    h_wrT = _ap(nc, "h_wrT", [1024, 1024])
    h_wkT = _ap(nc, "h_wkT", [1024, 1024])
    h_wvT = _ap(nc, "h_wvT", [1024, 1024])
    h_mur = _ap(nc, "h_mur", [128, 8])
    h_muk = _ap(nc, "h_muk", [128, 8])
    wqkv = _ap(nc, "wqkv", [1024, 3072])
    wout = _ap(nc, "wout", [1024, 1024])
    out = _ap(nc, "out", [4096, 1024], F32, kind="ExternalOutput")
    mT = _ap(nc, "mT", [1024, 4096], BF16, kind="Internal")
    z_scr = _ap(nc, "z_scr", [4096, 1024], F32, kind="Internal")
    actT = _ap(nc, "actT", [4096, 4096], BF16, kind="Internal")
    srf = _ap(nc, "srf", [4096, 1024], BF16, kind="Internal")
    odil = _ap(nc, "odil", [4096, 1024], BF16, kind="Internal")
    x1 = _ap(nc, "x1", [4096, 1024], F32, kind="Internal")
    x2 = _ap(nc, "x2", [4096, 1024], F32, kind="Internal")
    x3 = _ap(nc, "x3", [4096, 1024], F32, kind="Internal")

    b = Builder(nc)
    b.open_scope()
    emit_rwkv_mixer(b, x_in, mT, wk, wv, wr, p1p["ln1w"], p1p["ln1b"],
                    p1p["tmk"], p1p["tmv"], p1p["tmr"], p1p["td"], p1p["tf"],
                    pref="A")
    b.close_scope()
    b.open_scope()
    emit_rwkv_ffn(b, x_in, mT, x1, wo, fwk, frc, fwv, p2p["ln2w"],
                  p2p["ln2b"], p2p["ftmk"], p2p["ftmr"], z_scr, actT, srf,
                  pref="F")
    b.close_scope()
    b.open_scope()
    emit_channel_ffn(b, x1, x2, g_wrT, g_wkT, g_wvT, g_mur, g_muk, pref="G")
    b.close_scope()
    b.open_scope()
    emit_l1_attn(b, x2, x3, wqkv, wout, odil, pref="L")
    b.close_scope()
    b.open_scope()
    emit_channel_ffn(b, x3, out, h_wrT, h_wkT, h_wvT, h_mur, h_muk, pref="H")
    b.close_scope()
    b.finalize()
    return nc


def _pack8(v):
    return np.ascontiguousarray(np.asarray(v, np.float32).reshape(8, 128).T)


def _cc(a):
    return np.ascontiguousarray(np.asarray(a, np.float32))


def kernel(**inp):
    NCORES = 8
    x = np.asarray(inp["L0_attn_x"], np.float32)
    shards = [np.ascontiguousarray(x[2 * c:2 * c + 2].reshape(4096, 1024))
              for c in range(NCORES)]
    shared = {
        "wk": _cc(inp["L0_attn_w_key"]), "wv": _cc(inp["L0_attn_w_value"]),
        "wr": _cc(inp["L0_attn_w_recept"]),
        "ln1w": _pack8(inp["L0_attn_ln1_w"]),
        "ln1b": _pack8(inp["L0_attn_ln1_b"]),
        "tmk": _pack8(inp["L0_attn_time_mix_k"]),
        "tmv": _pack8(inp["L0_attn_time_mix_v"]),
        "tmr": _pack8(inp["L0_attn_time_mix_r"]),
        "td": _pack8(inp["L0_attn_time_decay"]),
        "tf": _pack8(inp["L0_attn_time_first"]),
        "wo": _cc(inp["L0_attn_w_output"]), "fwk": _cc(inp["L0_attn_f_w_key"]),
        "frc": _cc(inp["L0_attn_f_w_recept"]),
        "fwv": _cc(inp["L0_attn_f_w_value"]),
        "ln2w": _pack8(inp["L0_attn_ln2_w"]),
        "ln2b": _pack8(inp["L0_attn_ln2_b"]),
        "ftmk": _pack8(inp["L0_attn_f_time_mix_k"]),
        "ftmr": _pack8(inp["L0_attn_f_time_mix_r"]),
        "g_wrT": _cc(np.asarray(inp["L0_ffn_W_r"]).T),
        "g_wkT": _cc(np.asarray(inp["L0_ffn_W_k"]).T),
        "g_wvT": _cc(np.asarray(inp["L0_ffn_W_v"]).T),
        "g_mur": _pack8(inp["L0_ffn_mu_r"]),
        "g_muk": _pack8(inp["L0_ffn_mu_k"]),
        "wqkv": _cc(inp["L1_attn_w_qkv"]), "wout": _cc(inp["L1_attn_w_out"]),
        "h_wrT": _cc(np.asarray(inp["L1_ffn_W_r"]).T),
        "h_wkT": _cc(np.asarray(inp["L1_ffn_W_k"]).T),
        "h_wvT": _cc(np.asarray(inp["L1_ffn_W_v"]).T),
        "h_mur": _pack8(inp["L1_ffn_mu_r"]),
        "h_muk": _pack8(inp["L1_ffn_mu_k"]),
    }
    nc = build_fused()
    res = run_bass_kernel_spmd(
        nc, [{"x_in": s, **shared} for s in shards],
        core_ids=list(range(NCORES)))
    out = np.concatenate(
        [r["out"].reshape(2, 2048, 1024) for r in res.results], axis=0)
    return np.ascontiguousarray(out.astype(np.float32))


# version 26
# speedup vs baseline: 1.0119x; 1.0119x over previous
import contextlib
import numpy as np
import concourse.bass as bass
import concourse.mybir as mybir

F32 = mybir.dt.float32
BF16 = mybir.dt.bfloat16
AF = mybir.ActivationFunctionType
ALU = mybir.AluOpType
AX = mybir.AxisListType

D = 1024
EPS = 1e-5


class Sem:
    __slots__ = ("h", "name", "n")

    def __init__(self, handle, name):
        self.h = handle
        self.name = name
        self.n = 0


class Builder:
    ENGS = ("sync", "vector", "scalar", "tensor", "gpsimd")

    def __init__(self, nc):
        self.nc = nc
        self.q = {e: [] for e in self.ENGS}
        self.stack = contextlib.ExitStack()
        self.n_inst = 0
        self.n_waits = 0
        self.auto = {e: self.sem("au_" + e[:2]) for e in self.ENGS}
        self.writer = {}
        self.readers = {}
        self.know = {e: {} for e in self.ENGS}
        self.scopes = []
        self.pool_sems = []

    def sem(self, name):
        h = self.stack.enter_context(self.nc.semaphore(name))
        return Sem(h, name)

    def sbuf(self, name, shape, dtype):
        st = self.scopes[-1] if self.scopes else self.stack
        return st.enter_context(self.nc.sbuf_tensor(name, shape, dtype))

    def psum(self, name, shape, dtype):
        st = self.scopes[-1] if self.scopes else self.stack
        return st.enter_context(self.nc.psum_tensor(name, shape, dtype))

    def open_scope(self):
        self.scopes.append(contextlib.ExitStack())

    def close_scope(self):
        self.full_barrier()
        self.scopes.pop().close()

    def full_barrier(self):
        deps = [(s, s.n) for s in self.auto.values() if s.n > 0]
        deps += [(s, s.n) for s in self.pool_sems if s.n > 0]
        for e in self.ENGS:
            self.emit(e, None, wait=tuple(deps), track=False)

    def _edges(self, eng, rd, wr, suppress_self_pe=False):
        need = {}

        def add(dep):
            if dep is None:
                return
            s, v, peng = dep
            if suppress_self_pe and peng == eng == "tensor":
                return
            if self.know[eng].get(s, 0) >= v:
                return
            if need.get(s, 0) < v:
                need[s] = v
        for k in rd:
            add(self.writer.get(k))
        for k in wr:
            add(self.writer.get(k))
            for dep in self.readers.get(k, ()):
                add(dep)
        return need

    def emit(self, eng, fn=None, rd=(), wr=(), wait=(), track=True,
             suppress_self_pe=False, inck=1, sem=None):
        need = self._edges(eng, rd, wr, suppress_self_pe)
        for s, v in wait:
            if self.know[eng].get(s, 0) < v and need.get(s, 0) < v:
                need[s] = v
        waits = tuple(need.items())
        for s, v in waits:
            self.know[eng][s] = v
        tok = None
        if track and fn is not None:
            s = sem if sem is not None else self.auto[eng]
            s.n += inck
            tok = (s, s.n)
        self.n_inst += 1
        self.n_waits += len(waits)
        target = tok

        def play(e):
            for s_, v_ in waits:
                e.wait_ge(s_.h, v_)
            if fn is not None:
                ins = fn(e)
                if target is not None:
                    ins.then_inc(target[0].h, inck)
        self.q[eng].append(play)
        if tok is not None:
            dep = (tok[0], tok[1], eng)
            for k in wr:
                self.writer[k] = dep
                self.readers[k] = []
            for k in rd:
                self.readers.setdefault(k, []).append(dep)
        return tok

    def v(self, fn=None, **kw):
        return self.emit("vector", fn, **kw)

    def s(self, fn=None, **kw):
        return self.emit("scalar", fn, **kw)

    def t(self, fn=None, **kw):
        return self.emit("tensor", fn, **kw)

    def g(self, fn=None, **kw):
        return self.emit("gpsimd", fn, **kw)

    def y(self, fn=None, **kw):
        return self.emit("sync", fn, **kw)

    def barrier_wait(self, eng, deps):
        self.emit(eng, None, wait=deps, track=False)

    def warmup(self, bank_ap, bank_key, lhsT, rhs, lhs_key, rhs_key, n=20):
        for i in range(n):
            self.emit("tensor",
                      (lambda t, i=i: t.matmul(bank_ap, lhsT, rhs,
                                               start=True, stop=True)),
                      rd=(lhs_key, rhs_key),
                      wr=((bank_key,) if i in (0, n - 1) else ()),
                      track=(i == n - 1), suppress_self_pe=True)

    track_all_mm = True

    def acc_mm(self, bank_key, fn, rdk, first, last, pending):
        if first:
            pending.clear()
        for k in rdk:
            if k not in pending:
                pending.append(k)
        return self.emit("tensor", fn,
                         rd=(tuple(pending) if last else rdk),
                         wr=((bank_key,) if (first or last) else ()),
                         track=(last or self.track_all_mm),
                         suppress_self_pe=True)

    def mm_group(self, bank_key, items, sem=None):
        n = len(items)
        allrd = []
        seen = set()
        for _, rdk in items:
            for k in rdk:
                if k not in seen:
                    seen.add(k)
                    allrd.append(k)
        for j, (fn, rdk) in enumerate(items):
            self.emit("tensor", fn,
                      rd=(rdk if j < n - 1 else tuple(allrd)),
                      wr=((bank_key,) if j in (0, n - 1) else ()),
                      track=(j == n - 1 or self.track_all_mm),
                      suppress_self_pe=True, sem=sem)

    def finalize(self):
        while self.scopes:
            self.scopes.pop().close()
        nc = self.nc
        with nc.Block() as block:
            @block.sync
            def _(e):
                for f in self.q["sync"]:
                    f(e)

            @block.vector
            def _(e):
                for f in self.q["vector"]:
                    f(e)

            @block.scalar
            def _(e):
                for f in self.q["scalar"]:
                    f(e)

            @block.tensor
            def _(e):
                for f in self.q["tensor"]:
                    f(e)

            @block.gpsimd
            def _(e):
                for f in self.q["gpsimd"]:
                    f(e)
        self.stack.close()
        return nc


class DmaPool:

    def __init__(self, b, name, n, eng="sync"):
        self.b = b
        self.eng = eng
        self.sems = [b.sem(f"{name}{i}") for i in range(n)]
        b.pool_sems.extend(self.sems)
        self.i = 0

    def dma(self, out, in_, rd=(), wr=(), eng=None):
        b = self.b
        s = self.sems[self.i]
        self.i = (self.i + 1) % len(self.sems)
        return b.emit(eng or self.eng,
                      (lambda e, out=out, in_=in_: e.dma_start(out=out,
                                                               in_=in_)),
                      rd=rd, wr=wr, wait=((s, s.n),), inck=16, sem=s)

    def drain(self):
        return tuple((s, s.n) for s in self.sems)



def emit_identity(b, ident, key="ident"):
    b.g(lambda g: g.memset(ident[:], 0.0), wr=(key,))
    b.g(lambda g: g.affine_select(out=ident[:], in_=ident[:],
                                  compare_op=ALU.not_equal, fill=1.0, base=0,
                                  pattern=[[-1, 128]], channel_multiplier=1),
        rd=(key,), wr=(key,))


def emit_weightT_load_seq(b, pool, loads, wstage, wstage_key="wstage"):
    ns = wstage[:].shape[1]
    stage_key = "sbuf:" + wstage[:].tensor.name
    idx = 0
    for wt_dram, wbf, key in loads:
        src = wt_dram.rearrange("(a p) c -> p a c", p=128)
        for a in range(8):
            sl = idx % ns
            idx += 1
            pool.dma(wstage[:, sl:sl + 1, :], src[:, a:a + 1, :],
                     wr=((stage_key, sl),))
            b.v(lambda v, a=a, sl=sl, wbf=wbf: v.tensor_copy(
                wbf[:, a:a + 1, :], wstage[:, sl:sl + 1, :]),
                rd=((stage_key, sl),), wr=((key, a // 2),))


def emit_ln_slab(b, tiles, tile_keys, stats, scr, n=D, pref=""):
    nt = len(tiles)
    nh = n // 2
    kst = pref + "stats"
    K = pref + "scr"

    def fld(i):
        return scr[:, i * nt:(i + 1) * nt]

    for i, xap in enumerate(tiles):
        b.v(lambda v, xap=xap, i=i: v.bn_stats(stats[:, 12 * i:12 * i + 6],
                                               xap[:, 0:nh]),
            rd=(tile_keys[i],), wr=((kst, i, 0),))
        b.v(lambda v, xap=xap, i=i: v.bn_stats(stats[:, 12 * i + 6:12 * i + 12],
                                               xap[:, nh:n]),
            rd=(tile_keys[i],), wr=((kst, i, 1),))
    allst = tuple((kst, i, j) for i in range(nt) for j in range(2))
    st = lambda j: stats[:, j:12 * nt:12]
    m, ve, t1, t2, q, sq = (fld(i) for i in range(6))
    b.v(lambda v: v.tensor_add(t1, st(1), st(4)), rd=allst, wr=((K, "t1"),))
    b.v(lambda v: v.tensor_add(t2, st(7), st(10)), rd=allst, wr=((K, "t2"),))
    b.v(lambda v: v.tensor_add(t1, t1, t2), rd=((K, "t1"), (K, "t2")),
        wr=((K, "t1"),))
    b.v(lambda v: v.tensor_scalar_mul(m, t1, 0.25), rd=((K, "t1"),),
        wr=((K, "m"),))
    b.v(lambda v: v.tensor_mul(q, st(1), st(1)), rd=allst, wr=((K, "q"),))
    b.v(lambda v: v.tensor_mul(t2, st(4), st(4)), rd=allst, wr=((K, "t2"),))
    b.v(lambda v: v.tensor_add(q, q, t2), rd=((K, "q"), (K, "t2")),
        wr=((K, "q"),))
    b.v(lambda v: v.tensor_mul(t2, st(7), st(7)), rd=allst, wr=((K, "t2"),))
    b.v(lambda v: v.tensor_add(q, q, t2), rd=((K, "q"), (K, "t2")),
        wr=((K, "q"),))
    b.v(lambda v: v.tensor_mul(t2, st(10), st(10)), rd=allst, wr=((K, "t2"),))
    b.v(lambda v: v.tensor_add(q, q, t2), rd=((K, "q"), (K, "t2")),
        wr=((K, "q"),))
    b.v(lambda v: v.tensor_add(t1, st(2), st(5)), rd=allst, wr=((K, "t1"),))
    b.v(lambda v: v.tensor_add(t2, st(8), st(11)), rd=allst, wr=((K, "t2"),))
    b.v(lambda v: v.tensor_add(t1, t1, t2), rd=((K, "t1"), (K, "t2")),
        wr=((K, "t1"),))
    b.v(lambda v: v.tensor_scalar(ve, q, 0.25, None, ALU.mult),
        rd=((K, "q"),), wr=((K, "ve"),))
    b.v(lambda v: v.scalar_tensor_tensor(t2, t1, 1.0 / n, ve, ALU.mult,
                                         ALU.add),
        rd=((K, "t1"), (K, "ve")), wr=((K, "t2"),))
    b.v(lambda v: v.tensor_mul(t1, m, m), rd=((K, "m"),), wr=((K, "t1"),))
    b.v(lambda v: v.tensor_sub(ve, t2, t1), rd=((K, "t1"), (K, "t2")),
        wr=((K, "ve"),))
    b.v(lambda v: v.tensor_scalar(ve, ve, EPS, None, ALU.add),
        rd=((K, "ve"),), wr=((K, "ve"),))
    b.s(lambda s: s.activation(sq, ve, AF.Sqrt), rd=((K, "ve"),),
        wr=((K, "sq"),))
    rstd = q
    nb = t2
    b.v(lambda v: v.reciprocal(rstd, sq), rd=((K, "sq"),), wr=((K, "q"),))
    b.v(lambda v: v.scalar_tensor_tensor(nb, m, -1.0, rstd, ALU.mult,
                                         ALU.mult),
        rd=((K, "m"), (K, "q")), wr=((K, "t2"),))
    return rstd, nb, ((K, "q"), (K, "t2"))




def emit_lnT_seq(b, P, s, xin_rows, xt, ybf, yT, stats, scr, pT, ident,
                 ident_key, KXT, KYB, KYT, pin, affine=None, row0=None,
                 double=False, nt=4, ytk=None):
    base = s * 16 if row0 is None else row0
    nsl = 16 // nt
    if ytk is not None:
        assert affine is None
        allk = tuple(dict.fromkeys(ytk(sl) for sl in range(nsl)))
        b.v(lambda v: v.memset(yT[:, :, 0:1], 0.0), wr=allk)
    else:
        b.v(lambda v: v.memset(yT[:, :, 0:1], 0.0), wr=(KYT,))

    def loads(sl):
        pr = sl % 2
        for i in range(nt):
            row = base + sl * nt + i
            pin.dma(xt[:, pr * nt + i, :], xin_rows[row],
                    wr=(KXT(pr * nt + i),))

    ybf_par = ybf[:].shape[1] >= 2 * nt
    st_par = len(stats[:].shape) == 3

    loads(0)
    for sl in range(nsl):
        if sl + 1 < nsl:
            loads(sl + 1)
        pr = sl % 2
        yo = pr * nt if ybf_par else 0
        xs = lambda i, pr=pr: xt[:, pr * nt + i, :]
        stp = stats[:, pr] if st_par else stats
        scp = scr[:, pr] if st_par else scr
        prefp = P + f"s{pr}" if st_par else P
        rstd, nb, rnkeys = emit_ln_slab(
            b, [xs(i) for i in range(nt)],
            [KXT(pr * nt + i) for i in range(nt)], stp, scp, pref=prefp)
        for i in range(nt):
            b.s(lambda sx, i=i, rstd=rstd, nb=nb, xs=xs, yo=yo: sx.activation(
                ybf[:, yo + i, :], xs(i), AF.Identity,
                bias=nb[:, i:i + 1], scale=rstd[:, i:i + 1]),
                rd=(KXT(pr * nt + i),) + rnkeys, wr=(KYB(yo + i),))
        if double:
            rstd, nb, rnkeys = emit_ln_slab(
                b, [ybf[:, yo + i, :] for i in range(nt)],
                [KYB(yo + i) for i in range(nt)], stp, scp, pref=prefp)
            for i in range(nt):
                b.s(lambda sx, i=i, rstd=rstd, nb=nb, yo=yo: sx.activation(
                    ybf[:, yo + i, :], ybf[:, yo + i, :], AF.Identity,
                    bias=nb[:, i:i + 1], scale=rstd[:, i:i + 1]),
                    rd=(KYB(yo + i),) + rnkeys, wr=(KYB(yo + i),))
        for cb in range(8):
            bank = cb % 2
            items = []
            for i in range(nt):
                items.append((
                    (lambda t, cb=cb, i=i, bank=bank, yo=yo: t.transpose(
                        pT[bank][:, i, :],
                        ybf[:, yo + i, cb * 128:cb * 128 + 128], ident[:])),
                    (KYB(yo + i), ident_key)))
            b.mm_group((P + "pT", bank), items)
            b.s(lambda sx, cb=cb, bank=bank, sl=sl, nt=nt: sx.copy(
                yT[:, cb, 1 + sl * nt * 128: 1 + (sl + 1) * nt * 128],
                pT[bank][:, 0:nt, :].rearrange("p a q -> p (a q)")),
                rd=((P + "pT", bank),),
                wr=(ytk(sl),) if ytk is not None else (KYT,))
    if affine is not None:
        w8, b8, wkey, bkey = affine
        for cb in range(8):
            b.v(lambda v, cb=cb: v.tensor_scalar(
                yT[:, cb, 1:2049], yT[:, cb, 1:2049],
                w8[:, cb:cb + 1], b8[:, cb:cb + 1], ALU.mult, ALU.add),
                rd=(KYT, wkey, bkey), wr=(KYT,))


def emit_channel_ffn(b, x_in, x_out, wrT, wkT, wvT, mu_r8, mu_k8, pref=""):
    P = pref
    wr_bf = b.sbuf(P + "wr", [128, 8, 1024], BF16)
    wk_bf = b.sbuf(P + "wk", [128, 8, 1024], BF16)
    wv_bf = b.sbuf(P + "wv", [128, 8, 1024], BF16)
    wstage = b.sbuf(P + "wstage", [128, 2, 1024], F32)
    mur = b.sbuf(P + "smur", [128, 8], F32)
    muk = b.sbuf(P + "smuk", [128, 8], F32)
    ident = b.sbuf(P + "ident", [128, 128], BF16)
    yT = b.sbuf(P + "yT", [128, 8, 2049], BF16)
    ybf = b.sbuf(P + "ybf", [128, 4, 1024], BF16)
    xt = b.sbuf(P + "xt", [128, 8, 1024], F32)
    stats = b.sbuf(P + "stats", [128, 2, 48], F32)
    scr = b.sbuf(P + "scr", [128, 2, 24], F32)
    diffb = b.sbuf(P + "diff", [128, 512], BF16)
    mixr = b.sbuf(P + "mixr", [128, 8, 512], BF16)
    mixk = b.sbuf(P + "mixk", [128, 8, 512], BF16)
    actb = b.sbuf(P + "act", [128, 8, 512], BF16)
    srb = b.sbuf(P + "sr", [128, 4, 1024], F32)
    outf = b.sbuf(P + "outf", [128, 2, 4, 1024], F32)
    relsc = b.sbuf(P + "relsc", [128, 512], F32)
    pT = [b.psum(P + f"pT{i}", [128, 8, 128], BF16) for i in range(2)]
    pR = [b.psum(P + f"pR{i}", [128, 512], F32) for i in range(2)]
    pK = [b.psum(P + f"pK{i}", [128, 512], F32) for i in range(2)]
    pV = [b.psum(P + f"pV{i}", [128, 512], F32) for i in range(2)]

    pin = DmaPool(b, P + "pi", 8)
    pout = DmaPool(b, P + "po", 4)

    emit_identity(b, ident, P + "ident")
    emit_weightT_load_seq(b, pin, [(wrT, wr_bf, P + "wr"),
                                   (wkT, wk_bf, P + "wk"),
                                   (wvT, wv_bf, P + "wv")],
                          wstage, P + "wstage")
    pin.dma(mur[:], mu_r8, wr=(P + "mur",))
    pin.dma(muk[:], mu_k8, wr=(P + "muk",))

    xin_rows = x_in.rearrange("(r p) c -> r p c", p=128)
    xout_rows = x_out.rearrange("(r p) c -> r p c", p=128)

    KXT = lambda i: (P + "xt", i)
    KYB = lambda i: (P + "ybf", i)
    KYT = P + "yT"
    KMIXR = lambda cb: (P + "mixr", cb)
    KMIXK = lambda cb: (P + "mixk", cb)
    KACT = lambda dt: (P + "act", dt)
    KSR = lambda mt: (P + "sr", mt)
    KOUTF = lambda pr, i: (P + "outf", pr, i)
    KLNB = lambda i: (P + "lnb", i)

    for s in range(2):
        b.v(lambda v: v.memset(yT[:, :, 0:1], 0.0),
            wr=tuple((KYT, k) for k in range(4)))
        def loads_pa(sl, s=s):
            pr = sl % 2
            for i in range(4):
                row = s * 16 + sl * 4 + i
                pin.dma(xt[:, pr * 4 + i, :], xin_rows[row],
                        wr=(KXT(pr * 4 + i),))

        loads_pa(0)
        for sl in range(4):
            if sl + 1 < 4:
                loads_pa(sl + 1)
            pr = sl % 2
            rstd, nb, rnkeys = emit_ln_slab(
                b, [xt[:, pr * 4 + i, :] for i in range(4)],
                [KXT(pr * 4 + i) for i in range(4)], stats[:, pr],
                scr[:, pr], pref=P + f"s{pr}")
            for i in range(4):
                b.s(lambda sx, i=i, rstd=rstd, nb=nb, pr=pr: sx.activation(
                    ybf[:, i, :], xt[:, pr * 4 + i, :], AF.Identity,
                    bias=nb[:, i:i + 1], scale=rstd[:, i:i + 1]),
                    rd=(KXT(pr * 4 + i),) + rnkeys, wr=(KYB(i),))
            for cb in range(8):
                bank = cb % 2
                items = []
                for i in range(4):
                    items.append((
                        (lambda t, cb=cb, i=i, bank=bank: t.transpose(
                            pT[bank][:, i, :],
                            ybf[:, i, cb * 128:cb * 128 + 128], ident[:])),
                        (KYB(i), P + "ident")))
                b.mm_group((P + "pT", bank), items)
                b.s(lambda sx, cb=cb, bank=bank, sl=sl: sx.copy(
                    yT[:, cb, 1 + sl * 512: 1 + sl * 512 + 512],
                    pT[bank][:, 0:4, :].rearrange("p a q -> p (a q)")),
                    rd=((P + "pT", bank),), wr=((KYT, sl),))
        def front(ch):
            off = ch * 512
            pr = ch % 2
            ytks = ((KYT, ch),) + (((KYT, ch - 1),) if ch > 0 else ())
            for cb in range(8):
                b.v(lambda v, cb=cb, off=off: v.tensor_sub(
                    diffb[:], yT[:, cb, 1 + off:1 + off + 512],
                    yT[:, cb, off:off + 512]),
                    rd=ytks, wr=(P + "diff",))
                b.v(lambda v, cb=cb, off=off: v.scalar_tensor_tensor(
                    mixr[:, cb, :], diffb[:], mur[:, cb:cb + 1],
                    yT[:, cb, off:off + 512], ALU.mult, ALU.add),
                    rd=(P + "diff", P + "mur") + ytks, wr=(KMIXR(cb),))
                b.v(lambda v, cb=cb, off=off: v.scalar_tensor_tensor(
                    mixk[:, cb, :], diffb[:], muk[:, cb:cb + 1],
                    yT[:, cb, off:off + 512], ALU.mult, ALU.add),
                    rd=(P + "diff", P + "muk") + ytks, wr=(KMIXK(cb),))
            for mt in range(4):
                for nh in range(2):
                    bank = (mt * 2 + nh) % 2
                    items = [
                        ((lambda t, mt=mt, nh=nh, ct=ct, bank=bank: t.matmul(
                            pR[bank][:],
                            mixr[:, ct, mt * 128:mt * 128 + 128],
                            wr_bf[:, ct, nh * 512:nh * 512 + 512],
                            start=(ct == 0), stop=(ct == 7))),
                         (KMIXR(ct), (P + "wr", ct // 2)))
                        for ct in range(8)]
                    b.mm_group((P + "pR", bank), items)
                    b.s(lambda sx, mt=mt, nh=nh, bank=bank: sx.activation(
                        srb[:, mt, nh * 512:nh * 512 + 512], pR[bank][:],
                        AF.Sigmoid),
                        rd=((P + "pR", bank),), wr=(KSR(mt),))
            for dt in range(8):
                bank = dt % 2
                items = [
                    ((lambda t, dt=dt, ct=ct, bank=bank: t.matmul(
                        pK[bank][:],
                        wk_bf[:, ct, dt * 128:dt * 128 + 128],
                        mixk[:, ct, :],
                        start=(ct == 0), stop=(ct == 7))),
                     (KMIXK(ct), (P + "wk", ct // 2)))
                    for ct in range(8)]
                b.mm_group((P + "pK", bank), items)
                b.v(lambda v, bank=bank: v.tensor_scalar_max(
                    relsc[:], pK[bank][:], 0.0),
                    rd=((P + "pK", bank),), wr=(P + "relsc",))
                b.v(lambda v, dt=dt: v.tensor_mul(actb[:, dt, :], relsc[:],
                                                  relsc[:]),
                    rd=(P + "relsc",), wr=(KACT(dt),))
            for mt in range(4):
                for nh in range(2):
                    bank = (mt * 2 + nh) % 2
                    items = [
                        ((lambda t, mt=mt, nh=nh, dt=dt, bank=bank: t.matmul(
                            pV[bank][:],
                            actb[:, dt, mt * 128:mt * 128 + 128],
                            wv_bf[:, dt, nh * 512:nh * 512 + 512],
                            start=(dt == 0), stop=(dt == 7))),
                         (KACT(dt), (P + "wv", dt // 2)))
                        for dt in range(8)]
                    b.mm_group((P + "pV", bank), items)
                    b.v(lambda v, mt=mt, nh=nh, bank=bank, pr=pr:
                        v.tensor_mul(
                            outf[:, pr, mt, nh * 512:nh * 512 + 512],
                            srb[:, mt, nh * 512:nh * 512 + 512], pV[bank][:]),
                        rd=((P + "pV", bank), KSR(mt)), wr=(KOUTF(pr, mt),))
        def tail(ch):
            pr = ch % 2
            for i in range(4):
                row = s * 16 + ch * 4 + i
                pin.dma(xt[:, pr * 4 + i, :], xin_rows[row],
                        wr=(KXT(pr * 4 + i),))
            rstd2, nb2, rnkeys2 = emit_ln_slab(
                b, [outf[:, pr, i, :] for i in range(4)],
                [KOUTF(pr, i) for i in range(4)], stats[:, pr],
                scr[:, pr], pref=P + f"s{pr}")
            for i in range(4):
                b.s(lambda sx, i=i, rstd2=rstd2, nb2=nb2, pr=pr:
                    sx.activation(
                        outf[:, pr, i, :], outf[:, pr, i, :], AF.Identity,
                        bias=nb2[:, i:i + 1], scale=rstd2[:, i:i + 1]),
                    rd=(KOUTF(pr, i),) + rnkeys2, wr=(KOUTF(pr, i),))
                b.v(lambda v, i=i, pr=pr: v.tensor_add(
                    outf[:, pr, i, :], outf[:, pr, i, :], xt[:, pr * 4 + i, :]),
                    rd=(KOUTF(pr, i), KXT(pr * 4 + i)), wr=(KOUTF(pr, i),))
                row = s * 16 + ch * 4 + i
                pout.dma(xout_rows[row], outf[:, pr, i, :],
                         rd=(KOUTF(pr, i),))

        for ch in range(4):
            front(ch)
            if ch > 0:
                tail(ch - 1)
        tail(3)
    b.barrier_wait("sync", pout.drain())
    return pout


def emit_rwkv_mixer(b, x_in, mT_out, w_key, w_value, w_recept,
                    ln1w8, ln1b8, tmk8, tmv8, tmr8, td8, tf8, pref="A"):
    P = pref
    CH = 256
    NCH = 8
    wk_bf = b.sbuf(P + "wk", [128, 8, 1024], BF16)
    wv_bf = b.sbuf(P + "wv", [128, 8, 1024], BF16)
    wr_bf = b.sbuf(P + "wr", [128, 8, 1024], BF16)
    wstage = b.sbuf(P + "wstage", [128, 2, 1024], F32)
    prm = b.sbuf(P + "prm", [128, 8, 8], F32)
    eu8 = b.sbuf(P + "eu8", [128, 8], F32)
    dk8 = b.sbuf(P + "dk8", [128, 8], F32)
    ident = b.sbuf(P + "ident", [128, 128], BF16)
    yT = b.sbuf(P + "yT", [128, 8, 2049], BF16)
    ybf = b.sbuf(P + "ybf", [128, 4, 1024], BF16)
    xt = b.sbuf(P + "xt", [128, 4, 1024], F32)
    stats = b.sbuf(P + "stats", [128, 2, 48], F32)
    scr = b.sbuf(P + "scr", [128, 2, 24], F32)
    diffc = b.sbuf(P + "diffc", [128, 2, 8, CH], BF16)
    mixb = b.sbuf(P + "mix", [128, 2, 8, CH], BF16)
    ek = b.sbuf(P + "ek", [128, 2, 8, CH], F32)
    d1n = b.sbuf(P + "d1n", [128, 2, 8, CH], F32)
    srT = b.sbuf(P + "srT", [128, 2, 8, CH], BF16)
    alph = b.sbuf(P + "alph", [128, 8, CH + 1], F32)
    beth = b.sbuf(P + "beth", [128, 8, CH + 1], F32)
    dkt = b.sbuf(P + "dkt", [128, 8, CH], F32)
    numt = b.sbuf(P + "numt", [128, 4, CH], F32)
    dent = b.sbuf(P + "dent", [128, 4, CH], F32)
    mTb = b.sbuf(P + "mTb", [128, 4, CH], BF16)
    pT = [b.psum(P + f"pT{i}", [128, 8, 128], BF16) for i in range(2)]
    pM = [b.psum(P + f"pM{i}", [128, 512], F32) for i in range(4)]

    pin = DmaPool(b, P + "pi", 8)
    pout = DmaPool(b, P + "po", 4)

    emit_identity(b, ident, P + "ident")
    emit_weightT_load_seq(b, pin, [(w_key, wk_bf, P + "wk"),
                                   (w_value, wv_bf, P + "wv"),
                                   (w_recept, wr_bf, P + "wr")],
                          wstage, P + "wstage")
    for j, prm_d in enumerate((ln1w8, ln1b8, tmk8, tmv8, tmr8, td8, tf8)):
        pin.dma(prm[:, j, :], prm_d, wr=((P + "prm", j),))
    b.s(lambda s: s.activation(eu8[:], prm[:, 6, :], AF.Exp),
        rd=((P + "prm", 6),), wr=(P + "eu8",))
    b.s(lambda s: s.activation(dk8[:], prm[:, 5, :], AF.Exp),
        rd=((P + "prm", 5),), wr=(P + "dk8",))
    b.v(lambda v: v.tensor_scalar_mul(dk8[:], dk8[:], -1.0),
        rd=(P + "dk8",), wr=(P + "dk8",))
    b.s(lambda s: s.activation(dk8[:], dk8[:], AF.Exp),
        rd=(P + "dk8",), wr=(P + "dk8",))
    b.v(lambda v: v.memset(dkt[:], 0.0),
        wr=tuple((P + "dkt", dt) for dt in range(8)))
    for dt in range(8):
        b.v(lambda v, dt=dt: v.tensor_scalar(
            dkt[:, dt, :], dkt[:, dt, :], 0.0, dk8[:, dt:dt + 1],
            ALU.mult, ALU.add),
            rd=(P + "dk8",), wr=((P + "dkt", dt),))

    xin_rows = x_in.rearrange("(r p) c -> r p c", p=128)
    mT_rows = mT_out.rearrange("(a p) t -> a p t", p=128)

    KXT = lambda i: (P + "xt", i)
    KYB = lambda i: (P + "ybf", i)
    KYT = P + "yT"
    KMIX = lambda pr, cb: (P + "mix", pr, cb)
    KEK = lambda pr, dt: (P + "ek", pr, dt)
    KD1 = lambda pr, dt: (P + "d1n", pr, dt)
    KAL = lambda dt: (P + "al", dt)
    KBE = lambda dt: (P + "be", dt)
    KSR = lambda pr, dt: (P + "srT", pr, dt)
    KM = lambda dt: (P + "mTb", dt % 2)

    wmats = {"k": (wk_bf, P + "wk"), "v": (wv_bf, P + "wv"),
             "r": (wr_bf, P + "wr")}

    def emit_front(s, ch):
        pr = ch % 2
        off = ch * CH
        for cb in range(8):
            b.v(lambda v, cb=cb, off=off, pr=pr: v.tensor_sub(
                diffc[:, pr, cb, :], yT[:, cb, 1 + off:1 + off + CH],
                yT[:, cb, off:off + CH]),
                rd=(KYT,), wr=((P + "diffc", pr, cb),))

        def mixes(prm_j, prm_key):
            for cb in range(8):
                b.v(lambda v, cb=cb, off=off, j=prm_j, pr=pr:
                    v.scalar_tensor_tensor(
                        mixb[:, pr, cb, :], diffc[:, pr, cb, :],
                        prm[:, j, cb:cb + 1],
                        yT[:, cb, off:off + CH], ALU.mult, ALU.add),
                    rd=((P + "diffc", pr, cb), KYT, prm_key),
                    wr=(KMIX(pr, cb),))

        def mm_into(dt, bank, wname):
            wbf, wkey = wmats[wname]
            items = [
                ((lambda t, dt=dt, ct=ct, bank=bank, wbf=wbf, pr=pr: t.matmul(
                    pM[bank][:, 0:CH],
                    wbf[:, ct, dt * 128:dt * 128 + 128],
                    mixb[:, pr, ct, :],
                    start=(ct == 0), stop=(ct == 7))),
                 (KMIX(pr, ct), (wkey, ct // 2)))
                for ct in range(8)]
            b.mm_group((P + "pM", bank), items)

        mixes(2, (P + "prm", 2))
        for dt in range(8):
            bank = dt % 2
            mm_into(dt, bank, "k")
            b.s(lambda sx, dt=dt, bank=bank, pr=pr: sx.activation(
                ek[:, pr, dt, :], pM[bank][:, 0:CH], AF.Exp),
                rd=((P + "pM", bank),), wr=(KEK(pr, dt),))
        mixes(3, (P + "prm", 3))
        for dt in range(8):
            bank = 2 + dt % 2
            mm_into(dt, bank, "v")
            b.v(lambda v, dt=dt, bank=bank, pr=pr: v.tensor_mul(
                d1n[:, pr, dt, :], ek[:, pr, dt, :], pM[bank][:, 0:CH]),
                rd=((P + "pM", bank), KEK(pr, dt)), wr=(KD1(pr, dt),))
        mixes(4, (P + "prm", 4))
        for dt in range(8):
            bank = dt % 2
            mm_into(dt, bank, "r")
            b.s(lambda sx, dt=dt, bank=bank, pr=pr: sx.activation(
                srT[:, pr, dt, :], pM[bank][:, 0:CH], AF.Sigmoid),
                rd=((P + "pM", bank),), wr=(KSR(pr, dt),))

    def emit_wkv(s, ch):
        pr = ch % 2
        gtok = s * 2048 + ch * CH
        KN = lambda dp: (P + "numt", dp)
        KDN = lambda dp: (P + "dent", dp)

        def front(dt):
            dp = dt % 4
            b.s(lambda sx, dt=dt, pr=pr, dp=dp: sx.activation(
                numt[:, dp, :], d1n[:, pr, dt, :], AF.Identity,
                scale=eu8[:, dt:dt + 1]),
                rd=(KD1(pr, dt), P + "eu8"), wr=(KN(dp),))
            b.s(lambda sx, dt=dt, pr=pr, dp=dp: sx.activation(
                dent[:, dp, :], ek[:, pr, dt, :], AF.Identity,
                scale=eu8[:, dt:dt + 1]),
                rd=(KEK(pr, dt), P + "eu8"), wr=(KDN(dp),))
            b.v(lambda v, dt=dt: v.tensor_copy(alph[:, dt, 0:1],
                                               alph[:, dt, CH:CH + 1]),
                rd=(KAL(dt),), wr=(KAL(dt),))
            b.v(lambda v, dt=dt: v.tensor_copy(beth[:, dt, 0:1],
                                               beth[:, dt, CH:CH + 1]),
                rd=(KBE(dt),), wr=(KBE(dt),))
            b.v(lambda v, dt=dt, pr=pr: v.scalar_tensor_tensor(
                d1n[:, pr, dt, 0:1], alph[:, dt, 0:1], dk8[:, dt:dt + 1],
                d1n[:, pr, dt, 0:1], ALU.mult, ALU.add),
                rd=(KAL(dt), P + "dk8", KD1(pr, dt)), wr=(KD1(pr, dt),))
            b.v(lambda v, dt=dt, pr=pr: v.scalar_tensor_tensor(
                ek[:, pr, dt, 0:1], beth[:, dt, 0:1], dk8[:, dt:dt + 1],
                ek[:, pr, dt, 0:1], ALU.mult, ALU.add),
                rd=(KBE(dt), P + "dk8", KEK(pr, dt)), wr=(KEK(pr, dt),))
            b.v(lambda v, dt=dt, pr=pr: v.tensor_tensor_scan(
                alph[:, dt, 1:CH + 1], dkt[:, dt, :], d1n[:, pr, dt, :], 0.0,
                ALU.mult, ALU.add),
                rd=((P + "dkt", dt), KD1(pr, dt)), wr=(KAL(dt),))
            b.v(lambda v, dt=dt, pr=pr: v.tensor_tensor_scan(
                beth[:, dt, 1:CH + 1], dkt[:, dt, :], ek[:, pr, dt, :], 0.0,
                ALU.mult, ALU.add),
                rd=((P + "dkt", dt), KEK(pr, dt)), wr=(KBE(dt),))
            b.v(lambda v, dt=dt, dp=dp: v.tensor_add(
                numt[:, dp, :], numt[:, dp, :], alph[:, dt, 0:CH]),
                rd=(KN(dp), KAL(dt)), wr=(KN(dp),))
            b.v(lambda v, dt=dt, dp=dp: v.tensor_add(
                dent[:, dp, :], dent[:, dp, :], beth[:, dt, 0:CH]),
                rd=(KDN(dp), KBE(dt)), wr=(KDN(dp),))

        def tail(dt):
            dp = dt % 4
            b.s(lambda sx, dp=dp: sx.activation(dent[:, dp, :],
                                                dent[:, dp, :], AF.Ln),
                rd=(KDN(dp),), wr=(KDN(dp),))
            b.s(lambda sx, dp=dp: sx.activation(dent[:, dp, :],
                                                dent[:, dp, :], AF.Exp,
                                                scale=-1.0),
                rd=(KDN(dp),), wr=(KDN(dp),))
            b.v(lambda v, dp=dp: v.tensor_mul(numt[:, dp, :], numt[:, dp, :],
                                              dent[:, dp, :]),
                rd=(KN(dp), KDN(dp)), wr=(KN(dp),))
            b.v(lambda v, dt=dt, pr=pr, dp=dp: v.tensor_mul(
                mTb[:, dt % 4, :], numt[:, dp, :], srT[:, pr, dt, :]),
                rd=(KN(dp), KSR(pr, dt)), wr=(KM(dt),))
            pout.dma(mT_rows[dt][:, gtok:gtok + CH], mTb[:, dt % 4, :],
                     rd=(KM(dt),))

        for dt in range(8):
            front(dt)
            if dt > 1:
                tail(dt - 2)
        tail(6)
        tail(7)

    for s in range(2):
        emit_lnT_seq(b, P, s, xin_rows, xt, ybf, yT, stats, scr, pT,
                     ident, P + "ident", KXT, KYB, KYT, pin,
                     affine=(prm[:, 0, :], prm[:, 1, :],
                             (P + "prm", 0), (P + "prm", 1)), double=True,
                     nt=2)
        b.v(lambda v: v.memset(alph[:, :, CH:CH + 1], 0.0),
            wr=tuple(KAL(dt) for dt in range(8)))
        b.v(lambda v: v.memset(beth[:, :, CH:CH + 1], 0.0),
            wr=tuple(KBE(dt) for dt in range(8)))
        for ch in range(NCH):
            emit_front(s, ch)
            if ch > 0:
                emit_wkv(s, ch - 1)
        emit_wkv(s, NCH - 1)
    b.barrier_wait("sync", pout.drain())
    return pout


def emit_wide_load(b, pool, dram, wbf, key, wstage, rows, cols):
    src = dram.rearrange("(a p) c -> p a c", p=128)
    na = rows // 128
    ncc = cols // 1024
    ns = wstage[:].shape[1]
    stage_key = "sbuf:" + wstage[:].tensor.name
    idx = 0
    for a in range(na):
        for cc in range(ncc):
            sl = idx % ns
            idx += 1
            pool.dma(wstage[:, sl:sl + 1, :],
                     src[:, a:a + 1, 1024 * cc:1024 * cc + 1024],
                     wr=((stage_key, sl),))
            b.v(lambda v, a=a, cc=cc, sl=sl, wbf=wbf: v.tensor_copy(
                wbf[:, a:a + 1, 1024 * cc:1024 * cc + 1024],
                wstage[:, sl:sl + 1, :]),
                rd=((stage_key, sl),), wr=((key, a // 2, cc),))


def wkeys(key, na=8, ncc=1, aw=2):
    return tuple((key, a2, cc) for a2 in range(na // aw) for cc in range(ncc))


def emit_rwkv_ffn(b, x_in, mT_in, x1_out, w_output, fwk_d, frcpt_d, fwv_d,
                  ln2w8, ln2b8, ftmk8, ftmr8, z_scr, actT_scr, srf_scr,
                  pref="F"):
    P = pref
    xin_rows = x_in.rearrange("(r p) c -> r p c", p=128)
    x1_rows = x1_out.rearrange("(r p) c -> r p c", p=128)
    z_rows = z_scr.rearrange("(r p) c -> r p c", p=128)
    srf_rows = srf_scr.rearrange("(r p) c -> r p c", p=128)
    mT_rows = mT_in.rearrange("(a p) t -> a p t", p=128)
    actT_rows = actT_scr.rearrange("(a p) t -> a p t", p=128)

    pin = DmaPool(b, P + "pi", 8)
    pout = DmaPool(b, P + "po", 4)

    b.open_scope()
    wo_bf = b.sbuf(P + "wo", [128, 8, 1024], BF16)
    wstage = b.sbuf(P + "wstage", [128, 4, 1024], F32)
    mTc = b.sbuf(P + "mTc", [128, 2, 8, 512], BF16)
    xt = b.sbuf(P + "xt", [128, 8, 1024], F32)
    zt = b.sbuf(P + "zt", [128, 2, 4, 1024], F32)
    stats = b.sbuf(P + "stats", [128, 2, 48], F32)
    scr = b.sbuf(P + "scr", [128, 2, 24], F32)
    pA = [b.psum(P + f"pA{i}", [128, 512], F32) for i in range(2)]
    emit_wide_load(b, pin, w_output, wo_bf, P + "wo", wstage, 1024, 1024)
    KXT = lambda i: (P + "xt", i)
    KZT = lambda pr, i: (P + "zt", pr, i)
    KMTC = lambda pr: (P + "mTc", pr)

    def loads_a1(u):
        pr = u % 2
        gtok = u * 512
        pin.dma(mTc[:, pr], mT_rows[:, :, gtok:gtok + 512].rearrange(
            "a p t -> p a t"), wr=(KMTC(pr),))
        for i in range(4):
            pin.dma(xt[:, pr * 4 + i, :], xin_rows[u * 4 + i],
                    wr=(KXT(pr * 4 + i),))

    loads_a1(0)
    for u in range(8):
        if u + 1 < 8:
            loads_a1(u + 1)
        pr = u % 2
        rstd, nb, rnk = emit_ln_slab(
            b, [xt[:, pr * 4 + i, :] for i in range(4)],
            [KXT(pr * 4 + i) for i in range(4)], stats[:, pr],
            scr[:, pr], pref=P + f"s{pr}")
        for i in range(4):
            b.s(lambda sx, i=i, rstd=rstd, nb=nb, pr=pr: sx.activation(
                zt[:, pr, i, :], xt[:, pr * 4 + i, :], AF.Identity,
                bias=nb[:, i:i + 1], scale=rstd[:, i:i + 1]),
                rd=(KXT(pr * 4 + i),) + rnk, wr=(KZT(pr, i),))
        for mt in range(4):
            for nh in range(2):
                bank = (mt * 2 + nh) % 2
                items = [
                    ((lambda t, mt=mt, nh=nh, dt=dt, bank=bank, pr=pr:
                      t.matmul(
                          pA[bank][:],
                          mTc[:, pr, dt, mt * 128:mt * 128 + 128],
                          wo_bf[:, dt, nh * 512:nh * 512 + 512],
                          start=(dt == 0), stop=(dt == 7))),
                     (KMTC(pr),) + wkeys(P + "wo"))
                    for dt in range(8)]
                b.mm_group((P + "pA", bank), items)
                b.v(lambda v, mt=mt, nh=nh, bank=bank, pr=pr: v.tensor_add(
                    zt[:, pr, mt, nh * 512:nh * 512 + 512],
                    zt[:, pr, mt, nh * 512:nh * 512 + 512], pA[bank][:]),
                    rd=((P + "pA", bank), KZT(pr, mt)), wr=(KZT(pr, mt),))
        for i in range(4):
            pout.dma(z_rows[u * 4 + i], zt[:, pr, i, :], rd=(KZT(pr, i),))
    b.close_scope()
    b.open_scope()
    fwk_bf = b.sbuf(P + "fwk", [128, 8, 4096], BF16)
    fr_bf = b.sbuf(P + "frc", [128, 8, 1024], BF16)
    wstage2 = b.sbuf(P + "wst2", [128, 4, 1024], F32)
    prm = b.sbuf(P + "prm", [128, 4, 8], F32)
    ident = b.sbuf(P + "ident", [128, 128], BF16)
    yT = b.sbuf(P + "yT", [128, 8, 2049], BF16)
    ybf = b.sbuf(P + "ybf", [128, 4, 1024], BF16)
    xt2 = b.sbuf(P + "xt2", [128, 8, 1024], F32)
    stats2 = b.sbuf(P + "st2", [128, 2, 48], F32)
    scr2 = b.sbuf(P + "sc2", [128, 2, 24], F32)
    diffb = b.sbuf(P + "diff", [128, 512], BF16)
    mixb = b.sbuf(P + "mix", [128, 8, 512], BF16)
    actb = b.sbuf(P + "actb", [128, 2, 512], BF16)
    srfb = b.sbuf(P + "srfb", [128, 4, 1024], BF16)
    relsc = b.sbuf(P + "relsc", [128, 512], F32)
    pT = [b.psum(P + f"pT{i}", [128, 8, 128], BF16) for i in range(2)]
    pB = [b.psum(P + f"pB{i}", [128, 512], F32) for i in range(2)]
    emit_identity(b, ident, P + "ident")
    emit_wide_load(b, pin, fwk_d, fwk_bf, P + "fwk", wstage2, 1024, 4096)
    emit_wide_load(b, pin, frcpt_d, fr_bf, P + "frc", wstage2, 1024, 1024)
    for j, prm_d in enumerate((ln2w8, ln2b8, ftmk8, ftmr8)):
        pin.dma(prm[:, j, :], prm_d, wr=((P + "prm", j),))
    KXT2 = lambda i: (P + "xt2", i)
    KYB = lambda i: (P + "ybf", i)
    KYT = P + "yT"
    KMIX = lambda cb: (P + "mix", cb)
    KSRF = lambda i: (P + "srfb", i)
    for s in range(2):
        emit_lnT_seq(b, P, s, z_rows, xt2, ybf, yT, stats2, scr2, pT,
                     ident, P + "ident", KXT2, KYB, KYT, pin,
                     affine=(prm[:, 0, :], prm[:, 1, :],
                             (P + "prm", 0), (P + "prm", 1)))
        for ch in range(4):
            off = ch * 512
            gtok = s * 2048 + off
            for cb in range(8):
                b.v(lambda v, cb=cb, off=off: v.tensor_sub(
                    diffb[:], yT[:, cb, 1 + off:1 + off + 512],
                    yT[:, cb, off:off + 512]),
                    rd=(KYT,), wr=(P + "diff",))
                b.v(lambda v, cb=cb, off=off: v.scalar_tensor_tensor(
                    mixb[:, cb, :], diffb[:], prm[:, 2, cb:cb + 1],
                    yT[:, cb, off:off + 512], ALU.mult, ALU.add),
                    rd=(P + "diff", KYT, (P + "prm", 2)), wr=(KMIX(cb),))
            for ft in range(32):
                bank = ft % 2
                items = [
                    ((lambda t, ft=ft, ct=ct, bank=bank: t.matmul(
                        pB[bank][:],
                        fwk_bf[:, ct, ft * 128:ft * 128 + 128],
                        mixb[:, ct, :],
                        start=(ct == 0), stop=(ct == 7))),
                     (KMIX(ct), (P + "fwk", ct // 2, ft // 8)))
                    for ct in range(8)]
                b.mm_group((P + "pB", bank), items)
                b.v(lambda v, bank=bank: v.tensor_scalar_max(
                    relsc[:], pB[bank][:], 0.0),
                    rd=((P + "pB", bank),), wr=(P + "relsc",))
                b.v(lambda v, ft=ft: v.tensor_mul(
                    actb[:, ft % 2, :], relsc[:], relsc[:]),
                    rd=(P + "relsc",), wr=((P + "actb", ft % 2),))
                pout.dma(actT_rows[ft][:, gtok:gtok + 512],
                         actb[:, ft % 2, :], rd=((P + "actb", ft % 2),))
            for cb in range(8):
                b.v(lambda v, cb=cb, off=off: v.tensor_sub(
                    diffb[:], yT[:, cb, 1 + off:1 + off + 512],
                    yT[:, cb, off:off + 512]),
                    rd=(KYT,), wr=(P + "diff",))
                b.v(lambda v, cb=cb, off=off: v.scalar_tensor_tensor(
                    mixb[:, cb, :], diffb[:], prm[:, 3, cb:cb + 1],
                    yT[:, cb, off:off + 512], ALU.mult, ALU.add),
                    rd=(P + "diff", KYT, (P + "prm", 3)), wr=(KMIX(cb),))
            for mt in range(4):
                for nh in range(2):
                    bank = (mt * 2 + nh) % 2
                    items = [
                        ((lambda t, mt=mt, nh=nh, ct=ct, bank=bank: t.matmul(
                            pB[bank][:],
                            mixb[:, ct, mt * 128:mt * 128 + 128],
                            fr_bf[:, ct, nh * 512:nh * 512 + 512],
                            start=(ct == 0), stop=(ct == 7))),
                         (KMIX(ct), (P + "frc", ct // 2, 0)))
                        for ct in range(8)]
                    b.mm_group((P + "pB", bank), items)
                    b.s(lambda sx, mt=mt, nh=nh, bank=bank: sx.activation(
                        srfb[:, mt, nh * 512:nh * 512 + 512], pB[bank][:],
                        AF.Sigmoid),
                        rd=((P + "pB", bank),), wr=(KSRF(mt),))
            for i in range(4):
                row = s * 16 + ch * 4 + i
                pout.dma(srf_rows[row], srfb[:, i, :], rd=(KSRF(i),))
    b.close_scope()
    b.open_scope()
    fwv_bf = b.sbuf(P + "fwv", [128, 32, 1024], BF16)
    wstage3 = b.sbuf(P + "wst3", [128, 2, 1024], F32)
    actc = b.sbuf(P + "actc", [128, 2, 32, 512], BF16)
    zc = b.sbuf(P + "zc", [128, 2, 4, 1024], F32)
    srfc = b.sbuf(P + "srfc", [128, 2, 4, 1024], BF16)
    xc = b.sbuf(P + "xc", [128, 4, 1024], F32)
    gsc = b.sbuf(P + "gsc", [128, 512], F32)
    stats3 = b.sbuf(P + "st3", [128, 2, 48], F32)
    scr3 = b.sbuf(P + "sc3", [128, 2, 24], F32)
    pC = [b.psum(P + f"pC{i}", [128, 512], F32) for i in range(2)]
    emit_wide_load(b, pin, fwv_d, fwv_bf, P + "fwv", wstage3, 4096, 1024)
    KZC = lambda pr, i: (P + "zc", pr, i)

    def loads(u):
        s, ch = divmod(u, 4)
        pr = u % 2
        gtok = s * 2048 + ch * 512
        pin.dma(actc[:, pr], actT_rows[:, :, gtok:gtok + 512].rearrange(
            "a p t -> p a t"), wr=((P + "actc", pr),))
        for i in range(4):
            row = s * 16 + ch * 4 + i
            pin.dma(zc[:, pr, i, :], z_rows[row], wr=(KZC(pr, i),))
            pin.dma(srfc[:, pr, i, :], srf_rows[row],
                    wr=((P + "srfc", pr, i),))

    def compute(u):
        s, ch = divmod(u, 4)
        pr = u % 2
        for i in range(4):
            row = s * 16 + ch * 4 + i
            pin.dma(xc[:, i, :], xin_rows[row], wr=((P + "xc", i),))
        for mt in range(4):
            for nh in range(2):
                bank = (mt * 2 + nh) % 2
                items = [
                    ((lambda t, mt=mt, nh=nh, ft=ft, bank=bank, pr=pr:
                      t.matmul(
                          pC[bank][:],
                          actc[:, pr, ft, mt * 128:mt * 128 + 128],
                          fwv_bf[:, ft, nh * 512:nh * 512 + 512],
                          start=(ft == 0), stop=(ft == 31))),
                     ((P + "actc", pr), (P + "fwv", ft, 0)))
                    for ft in range(32)]
                b.mm_group((P + "pC", bank), items)
                b.v(lambda v, mt=mt, nh=nh, bank=bank, pr=pr: v.tensor_mul(
                    gsc[:], srfc[:, pr, mt, nh * 512:nh * 512 + 512],
                    pC[bank][:]),
                    rd=((P + "pC", bank), (P + "srfc", pr, mt)),
                    wr=(P + "gsc",))
                b.v(lambda v, mt=mt, nh=nh, pr=pr: v.tensor_add(
                    zc[:, pr, mt, nh * 512:nh * 512 + 512],
                    zc[:, pr, mt, nh * 512:nh * 512 + 512], gsc[:]),
                    rd=(P + "gsc", KZC(pr, mt)), wr=(KZC(pr, mt),))
        rstd3, nb3, rnk3 = emit_ln_slab(
            b, [zc[:, pr, i, :] for i in range(4)],
            [KZC(pr, i) for i in range(4)], stats3[:, pr], scr3[:, pr],
            pref=P + f"Bs{pr}")
        for i in range(4):
            b.s(lambda sx, i=i, rstd3=rstd3, nb3=nb3, pr=pr: sx.activation(
                zc[:, pr, i, :], zc[:, pr, i, :], AF.Identity,
                bias=nb3[:, i:i + 1], scale=rstd3[:, i:i + 1]),
                rd=(KZC(pr, i),) + rnk3, wr=(KZC(pr, i),))
            b.v(lambda v, i=i, pr=pr: v.tensor_add(
                zc[:, pr, i, :], zc[:, pr, i, :], xc[:, i, :]),
                rd=(KZC(pr, i), (P + "xc", i)), wr=(KZC(pr, i),))
            row = (s * 16 + ch * 4 + i)
            pout.dma(x1_rows[row], zc[:, pr, i, :], rd=(KZC(pr, i),))

    loads(0)
    for u in range(8):
        if u < 7:
            loads(u + 1)
        compute(u)
    b.close_scope()
    b.barrier_wait("sync", pout.drain())
    return pout


def emit_l1_attn(b, x_in, x_out, wqkv_d, wout_d, odil_scr, pref="L"):
    P = pref
    xin_rows = x_in.rearrange("(r p) c -> r p c", p=128)
    xout_rows = x_out.rearrange("(r p) c -> r p c", p=128)
    odil_v = odil_scr.rearrange("(s i g) d -> s g i d", s=2, g=4)
    wq_v = wqkv_d.rearrange("(a p) f -> p a f", p=128)

    pin = DmaPool(b, P + "pi", 8)
    pout = DmaPool(b, P + "po", 4)
    ident = b.sbuf(P + "ident", [128, 128], BF16)
    emit_identity(b, ident, P + "ident")
    kT = b.sbuf(P + "kT", [128, 8, 2048], BF16)
    qT = b.sbuf(P + "qT", [128, 8, 2048], BF16)
    vaug = b.sbuf(P + "vaug", [128, 16, 1040], BF16)
    yT = b.sbuf(P + "yT", [128, 8, 2049], BF16)
    KKT = lambda dt: (P + "kT", dt)
    KQT = lambda dt: (P + "qT", dt)
    KVA = P + "vaug"
    KVT = lambda dt: (P + "vT", dt)
    KVD = P + "vdil"
    KYT = P + "yT"

    def _seq(s):
        S = P + str(s)
        b.open_scope()
        vdil = b.sbuf(S + "vdil", [128, 16, 1040], BF16)
        b.open_scope()
        vT = b.sbuf(S + "vT", [128, 8, 2048], BF16)
        b.open_scope()
        wstg = b.sbuf(S + "wstg", [128, 4, 128], F32)
        wblk = b.sbuf(S + "wblk", [128, 8, 128], BF16)
        ybf = b.sbuf(S + "ybf", [128, 1, 1024], BF16)
        xt = b.sbuf(S + "xt", [128, 2, 1024], F32)
        stats = b.sbuf(S + "st", [128, 2, 12], F32)
        scr = b.sbuf(S + "sc", [128, 2, 6], F32)
        pT = [b.psum(S + f"pT{i}", [128, 8, 128], BF16) for i in range(2)]
        pQ = [b.psum(S + f"pQ{i}", [128, 512], F32) for i in range(2)]
        KXT = lambda i: (P + "xt", i)
        KYB = lambda i: (P + "ybf", i)
        emit_lnT_seq(b, P, s, xin_rows, xt, ybf, yT, stats, scr, pT,
                     ident, P + "ident", KXT, KYB, KYT, pin, nt=1,
                     ytk=lambda sl: (P + "yT", sl // 4))
        KWB = P + "wblk"

        def load_wblk(col):
            for h in range(2):
                pin.dma(wstg[:], wq_v[:, h * 4:h * 4 + 4, col:col + 128],
                        wr=(P + "wstg",))
                b.v(lambda v, h=h: v.tensor_copy(
                    wblk[:, h * 4:h * 4 + 4, :], wstg[:]),
                    rd=(P + "wstg",), wr=(KWB,))

        for which, fofs, dst, dkey, scale in (
                ("k", 1024, kT, KKT, None), ("q", 0, qT, KQT, 0.125),
                ("v", 2048, vT, KVT, None)):
            for dt in range(8):
                load_wblk(fofs + dt * 128)
                for tc in range(4):
                    bank = tc % 2
                    items = [
                        ((lambda t, ct=ct, tc=tc, bank=bank: t.matmul(
                            pQ[bank][:],
                            wblk[:, ct, :],
                            yT[:, ct, 1 + tc * 512:1 + tc * 512 + 512],
                            start=(ct == 0), stop=(ct == 7))),
                         ((P + "yT", tc), KWB))
                        for ct in range(8)]
                    b.mm_group((P + "pQ", bank), items)
                    if scale is None:
                        b.s(lambda sx, dt=dt, tc=tc, bank=bank, dst=dst:
                            sx.copy(
                                dst[:, dt, tc * 512:tc * 512 + 512],
                                pQ[bank][:]),
                            rd=((P + "pQ", bank),), wr=((dkey(dt)),))
                    else:
                        b.s(lambda sx, dt=dt, tc=tc, bank=bank, dst=dst,
                            scale=scale: sx.activation(
                                dst[:, dt, tc * 512:tc * 512 + 512],
                                pQ[bank][:], AF.Identity, scale=scale),
                            rd=((P + "pQ", bank),), wr=((dkey(dt)),))
        b.v(lambda v: v.memset(vaug[:, :, 64:1040:65], 1.0), wr=(KVA,))
        for jt in range(16):
            for dtile in range(8):
                b.emit("tensor",
                       (lambda t, jt=jt, dtile=dtile: t.transpose(
                           pT[0][:, dtile, :],
                           vT[:, dtile, jt * 128:jt * 128 + 128], ident[:])),
                       rd=(KVT(dtile), P + "ident"),
                       wr=((P + "pT", 0),) if dtile in (0, 7) else (),
                       track=(dtile == 7), suppress_self_pe=True)
            b.s(lambda sx, jt=jt: sx.copy(
                vaug[:, jt, :].rearrange("p (h c) -> p h c",
                                         c=65)[:, :, 0:64],
                pT[0][:].rearrange("p a q -> p (a q)").rearrange(
                    "p (h c) -> p h c", c=64)),
                rd=((P + "pT", 0),), wr=(KVA,))
        b.close_scope()
        b.open_scope()
        pY = b.psum(S + "pY", [128, 128], BF16)
        b.v(lambda v: v.memset(vdil[:, :, 64:1040:65], 1.0), wr=(KVD,))
        for g in range(4):
            for jp in range(4):
                for dtile in range(8):
                    b.emit("tensor",
                           (lambda t, g=g, jp=jp, dtile=dtile: t.transpose(
                               pY[:], vT[:, dtile,
                                         g + 512 * jp:g + 512 * jp + 509:4],
                               ident[:])),
                           rd=(KVT(dtile), P + "ident"),
                           wr=((P + "pY",),), track=True,
                           suppress_self_pe=True)
                    b.s(lambda sx, g=g, jp=jp, dtile=dtile: sx.copy(
                        vdil[:, g * 4 + jp,
                             dtile * 130:dtile * 130 + 130].rearrange(
                            "p (h c) -> p h c", c=65)[:, :, 0:64],
                        pY[:].rearrange("p (h c) -> p h c", c=64)),
                        rd=((P + "pY",),), wr=(KVD,))
        b.close_scope()
        b.close_scope()
        b.open_scope()
        pscC = [b.psum(S + f"Cpsc{i}", [128, 2, 512], F32) for i in range(2)]
        poC = [b.psum(S + f"Cpo{i}", [128, 512], F32) for i in range(2)]
        pt2C = b.psum(S + "Cpt2", [128, 4, 66], BF16)
        ptbC = b.sbuf(S + "Cptb", [128, 2, 2, 512], BF16)
        obufC = b.sbuf(S + "Cobuf", [128, 2, 512], BF16)
        odil = b.sbuf(S + "odil", [128, 16, 1040], BF16)
        rscD = b.sbuf(S + "rscD", [128, 16, 16], F32)
        KPT = lambda i: (P + "ptb", i)
        KOB = lambda i: (P + "obuf", i)
        KOD = P + "odil"

        def emit_sc_dil(g, hp, jp):
            par = jp % 2
            for hb in range(2):
                b.t(lambda t, g=g, hp=hp, jp=jp, hb=hb, par=par: t.matmul(
                    pscC[par][:, hb, :],
                    kT[64 * hb:64 * hb + 64, hp,
                       g + 512 * jp:g + 512 * jp + 509:4],
                    qT[64 * hb:64 * hb + 64, hp, g::4],
                    start=True, stop=True,
                    tile_position=(64 * hb, 0)),
                    rd=(KKT(hp), KQT(hp)), wr=((P + "psc", par, hb),),
                    suppress_self_pe=True, track=True)
            b.s(lambda sx, par=par: sx.activation(
                ptbC[:, par].rearrange("p a q -> p (a q)"),
                pscC[par][:].rearrange("p a q -> p (a q)"), AF.Exp),
                rd=((P + "psc", par, 0), (P + "psc", par, 1)),
                wr=(KPT(par),))

        def emit_pv_dil(g, hp, jp):
            par = jp % 2
            for hb in range(2):
                h = hp * 2 + hb
                b.acc_mm((P + "po", hb),
                         (lambda t, g=g, jp=jp, h=h, hb=hb, par=par: t.matmul(
                             poC[hb][0:65, :],
                             vdil[:, g * 4 + jp, h * 65:h * 65 + 65],
                             ptbC[:, par, hb, :],
                             start=(jp == 0), stop=(jp == 3),
                             skip_group_check=True)),
                         (KVD, KPT(par)), jp == 0, jp == 3, pendC[hb])

        def tail_dil(g, hp):
            for hb in range(2):
                h = hp * 2 + hb
                b.v(lambda v, hb=hb: v.tensor_copy(
                    obufC[0:65, hb, :], poC[hb][0:65, :]),
                    rd=((P + "po", hb),), wr=(KOB(hb),))
            for hb in range(2):
                h = hp * 2 + hb
                items = [
                    ((lambda t, hb=hb, ib=ib: t.transpose(
                        pt2C[:, ib, 0:65], obufC[0:65, hb,
                                                 ib * 128:ib * 128 + 128],
                        ident[0:65, 0:65])),
                     (KOB(hb), P + "ident"))
                    for ib in range(4)]
                b.mm_group((P + "pt2",), items)
                for ib in range(4):
                    b.v(lambda v, g=g, ib=ib, h=h: v.tensor_copy(
                        odil[:, g * 4 + ib, h * 65:h * 65 + 65],
                        pt2C[:, ib, 0:65]),
                        rd=((P + "pt2",),), wr=(KOD,))

        b.warmup(pscC[0][:, 0, :], (P + "psc", 0, 0), ident[:],
                 kT[:, 0, 0:512], P + "ident", KKT(0))
        pendC = [[], []]
        units = [(g, hp) for g in range(4) for hp in range(8)]
        tail = None
        for g, hp in units:
            emit_sc_dil(g, hp, 0)
            for jp in range(4):
                if jp < 3:
                    emit_sc_dil(g, hp, jp + 1)
                if jp == 0 and tail is not None:
                    tail()
                    tail = None
                emit_pv_dil(g, hp, jp)
            tail = (lambda g=g, hp=hp: tail_dil(g, hp))
        tail()
        b.v(lambda v: v.reciprocal(rscD[:], odil[:, :, 64:1040:65]),
            rd=(KOD,), wr=(P + "rscD",))
        for sl in range(16):
            for h in range(16):
                b.v(lambda v, sl=sl, h=h: v.tensor_scalar(
                    odil[:, sl, h * 65:h * 65 + 64],
                    odil[:, sl, h * 65:h * 65 + 64],
                    rscD[:, sl, h:h + 1], 1.0 / 3.0,
                    ALU.mult, ALU.mult),
                    rd=(KOD, P + "rscD"), wr=(KOD,))
            pout.dma(odil_v[s, sl // 4,
                            (sl % 4) * 128:(sl % 4) * 128 + 128,
                            :].rearrange("i (h c) -> i h c", c=64),
                     odil[:, sl, :].rearrange("p (h c) -> p h c",
                                              c=65)[:, :, 0:64],
                     rd=(KOD,), wr=(P + "odildram",))
        b.close_scope()
        b.close_scope()
        b.open_scope()
        psc = [b.psum(S + f"Dpsc{i}", [128, 2, 512], F32) for i in range(2)]
        po = [b.psum(S + f"Dpo{i}", [128, 512], F32) for i in range(2)]
        pBF = b.psum(S + "DpBF", [128, 8, 128], BF16)
        pW = [psc[1][:, 0, :], psc[1][:, 1, :]]
        ptb = b.sbuf(S + "Dptb", [128, 2, 2, 512], BF16)
        obuf = b.sbuf(S + "Dobuf", [128, 2, 512], BF16)
        oglb = b.sbuf(S + "oglb", [128, 4, 1040], BF16)
        oloc = b.sbuf(S + "oloc", [128, 4, 1040], BF16)
        odc = b.sbuf(S + "odc", [128, 4, 1024], BF16)
        oTb = b.sbuf(S + "oTb", [128, 8, 512], BF16)
        wout = b.sbuf(S + "wout", [128, 8, 1024], BF16)
        xt4 = b.sbuf(S + "xt4", [128, 2, 1024], F32)
        att4 = b.sbuf(S + "att4", [128, 4, 1024], F32)
        stats4 = b.sbuf(S + "st4", [128, 48], F32)
        scr4 = b.sbuf(S + "sc4", [128, 24], F32)
        rsG = b.sbuf(S + "rsG", [128, 4, 16], F32)
        rsL = b.sbuf(S + "rsL", [128, 4, 16], F32)
        fen = b.sbuf(S + "Dfen", [128, 1], F32)
        emit_wide_load(b, pin, wout_d, wout, P + "wout", xt4, 1024, 1024)
        KPT = lambda i: (P + "ptb", i)
        KOB = lambda i: (P + "obuf", i)
        KOG = P + "oglb"
        KOL = P + "oloc"

        def emit_sc_glb(hp, it, jt):
            par = jt % 2
            for hb in range(2):
                b.t(lambda t, hp=hp, jt=jt, it=it, hb=hb, par=par: t.matmul(
                    psc[par][:, hb, :],
                    kT[64 * hb:64 * hb + 64, hp, jt * 128:jt * 128 + 128],
                    qT[64 * hb:64 * hb + 64, hp, it * 512:it * 512 + 512],
                    start=True, stop=True,
                    tile_position=(64 * hb, 0)),
                    rd=(KKT(hp), KQT(hp)), wr=((P + "psc", par, hb),),
                    suppress_self_pe=True, track=True)
            b.s(lambda sx, par=par: sx.activation(
                ptb[:, par].rearrange("p a q -> p (a q)"),
                psc[par][:].rearrange("p a q -> p (a q)"), AF.Exp),
                rd=((P + "psc", par, 0), (P + "psc", par, 1)),
                wr=(KPT(par),))

        def emit_pv_glb(hp, jt):
            par = jt % 2
            for hb in range(2):
                h = hp * 2 + hb
                b.acc_mm((P + "po", hb),
                         (lambda t, jt=jt, h=h, hb=hb, par=par: t.matmul(
                             po[hb][0:65, :],
                             vaug[:, jt, h * 65:h * 65 + 65],
                             ptb[:, par, hb, :],
                             start=(jt == 0), stop=(jt == 15),
                             skip_group_check=True)),
                         (KVA, KPT(par)), jt == 0, jt == 15, pendG[hb])

        def tail_glb(hp):
            for hb in range(2):
                b.v(lambda v, hb=hb: v.tensor_copy(
                    obuf[0:65, hb, :], po[hb][0:65, :]),
                    rd=((P + "po", hb),), wr=(KOB(hb), KOB(2 + hb)))
            for hb in range(2):
                h = hp * 2 + hb
                items = [
                    ((lambda t, hb=hb, ib=ib: t.transpose(
                        pBF[:, ib, 0:65], obuf[0:65, hb,
                                               ib * 128:ib * 128 + 128],
                        ident[0:65, 0:65])),
                     (KOB(hb), P + "ident"))
                    for ib in range(4)]
                b.mm_group((P + "pBF",), items)
                for ib in range(4):
                    b.v(lambda v, ib=ib, h=h: v.tensor_copy(
                        oglb[:, ib, h * 65:h * 65 + 65], pBF[:, ib, 0:65]),
                        rd=((P + "pBF",),), wr=(KOG,))

        def emit_sc_loc(hp, it, ib):
            w = it * 4 + ib
            par = ib % 2
            for hb in range(2):
                b.t(lambda t, hp=hp, w=w, hb=hb, par=par: t.matmul(
                    psc[par][:, hb, 0:128],
                    kT[64 * hb:64 * hb + 64, hp, w * 128:w * 128 + 128],
                    qT[64 * hb:64 * hb + 64, hp, w * 128:w * 128 + 128],
                    start=True, stop=True,
                    tile_position=(64 * hb, 0)),
                    rd=(KKT(hp), KQT(hp)), wr=((P + "psc", par, hb),),
                    suppress_self_pe=True, track=True)
            b.s(lambda sx, par=par: sx.activation(
                ptb[:, par, :, 0:128], psc[par][:, :, 0:128], AF.Exp),
                rd=((P + "psc", par, 0), (P + "psc", par, 1)),
                wr=(KPT(par),))

        def tail_loc(hp, it, ib):
            par = ib % 2
            co = 256 * par
            for hb in range(2):
                h = hp * 2 + hb
                b.t(lambda t, w=it * 4 + ib, h=h, hb=hb, par=par, co=co:
                    t.matmul(
                        po[hb][0:65, co:co + 128],
                        vaug[:, w, h * 65:h * 65 + 65],
                        ptb[:, par, hb, 0:128],
                        start=True, stop=True),
                    rd=(KVA, KPT(par)),
                    wr=((P + "po", hb, par), (P + "po", hb)),
                    suppress_self_pe=True, track=True)
            for hb in range(2):
                b.v(lambda v, hb=hb, par=par, co=co: v.tensor_copy(
                    obuf[0:65, hb, co:co + 128], po[hb][0:65, co:co + 128]),
                    rd=((P + "po", hb, par),), wr=(KOB(par * 2 + hb),))
            for hb in range(2):
                h = hp * 2 + hb
                b.emit("tensor",
                       (lambda t, hb=hb, par=par, co=co: t.transpose(
                           pBF[:, 4 + hb, 0:65], obuf[0:65, hb, co:co + 128],
                           ident[0:65, 0:65])),
                       rd=(KOB(par * 2 + hb), P + "ident"),
                       wr=((P + "pBF",),),
                       track=True, suppress_self_pe=True)
                b.v(lambda v, ib=ib, h=h, hb=hb: v.tensor_copy(
                    oloc[:, ib, h * 65:h * 65 + 65], pBF[:, 4 + hb, 0:65]),
                    rd=((P + "pBF",),), wr=(KOL,))

        def it_tail(it):
                pin.dma(odc[:],
                        odil_scr.rearrange("(r p) c -> r p c", p=128)[
                            s * 16 + it * 4:s * 16 + it * 4 + 4].rearrange(
                            "r p c -> p r c"),
                        rd=(P + "odildram",), wr=(P + "odc",))
                for buf, key, rbuf, rkey in ((oglb, KOG, rsG, P + "rsG"),
                                             (oloc, KOL, rsL, P + "rsL")):
                    b.v(lambda v, buf=buf, rbuf=rbuf: v.reciprocal(
                        rbuf[:], buf[:, :, 64:1040:65]),
                        rd=(key,), wr=(rkey,))
                for ib in range(4):
                    for h in range(16):
                        b.v(lambda v, ib=ib, h=h: v.tensor_scalar(
                            oloc[:, ib, h * 65:h * 65 + 64],
                            oloc[:, ib, h * 65:h * 65 + 64],
                            rsL[:, ib, h:h + 1], 1.0 / 3.0,
                            ALU.mult, ALU.mult),
                            rd=(KOL, P + "rsL"), wr=(KOL,))
                        b.v(lambda v, ib=ib, h=h: v.tensor_scalar(
                            oglb[:, ib, h * 65:h * 65 + 64],
                            oglb[:, ib, h * 65:h * 65 + 64],
                            rsG[:, ib, h:h + 1], 1.0 / 3.0,
                            ALU.mult, ALU.mult),
                            rd=(KOG, P + "rsG"), wr=(KOG,))
                    b.v(lambda v, ib=ib: v.tensor_add(
                        odc[:, ib, :].rearrange("p (h c) -> p h c", c=64),
                        odc[:, ib, :].rearrange("p (h c) -> p h c", c=64),
                        oloc[:, ib, :].rearrange("p (h c) -> p h c",
                                                 c=65)[:, :, 0:64]),
                        rd=(P + "odc", KOL), wr=(P + "odc",))
                    b.v(lambda v, ib=ib: v.tensor_add(
                        odc[:, ib, :].rearrange("p (h c) -> p h c", c=64),
                        odc[:, ib, :].rearrange("p (h c) -> p h c", c=64),
                        oglb[:, ib, :].rearrange("p (h c) -> p h c",
                                                 c=65)[:, :, 0:64]),
                        rd=(P + "odc", KOG), wr=(P + "odc",))
                for ib in range(4):
                    items = [
                        ((lambda t, ib=ib, cb=cb: t.transpose(
                            pBF[:, cb, :], odc[:, ib, cb * 128:cb * 128 + 128],
                            ident[:])),
                         (P + "odc", P + "ident"))
                        for cb in range(8)]
                    b.mm_group((P + "pBF",), items)
                    for cb in range(8):
                        b.v(lambda v, ib=ib, cb=cb: v.tensor_copy(
                            oTb[:, cb, ib * 128:ib * 128 + 128], pBF[:, cb, :]),
                            rd=((P + "pBF",),), wr=((P + "oTb", cb),))
                for mt in range(4):
                    for nh in range(2):
                        bank = (mt * 2 + nh) % 2
                        items = [
                            ((lambda t, mt=mt, nh=nh, dt=dt, bank=bank: t.matmul(
                                pW[bank],
                                oTb[:, dt, mt * 128:mt * 128 + 128],
                                wout[:, dt, nh * 512:nh * 512 + 512],
                                start=(dt == 0), stop=(dt == 7))),
                             ((P + "oTb", dt),) + wkeys(P + "wout", aw=1))
                            for dt in range(8)]
                        b.mm_group((P + "psc", 1, bank), items)
                        b.v(lambda v, mt=mt, nh=nh, bank=bank: v.tensor_copy(
                            att4[:, mt, nh * 512:nh * 512 + 512], pW[bank]),
                            rd=((P + "psc", 1, bank),), wr=((P + "att4", mt),))
                xk4 = "sbuf:" + xt4[:].tensor.name
                for i in range(2):
                    row = s * 16 + it * 4 + i
                    pin.dma(xt4[:, i, :], xin_rows[row],
                            wr=((P + "xt4", i), (xk4, i)))
                rstd4, nb4, rnk4 = emit_ln_slab(
                    b, [att4[:, i, :] for i in range(4)],
                    [(P + "att4", i) for i in range(4)], stats4, scr4,
                    pref=P + "D")
                for i in range(4):
                    row = s * 16 + it * 4 + i
                    b.s(lambda sx, i=i, rstd4=rstd4, nb4=nb4: sx.activation(
                        att4[:, i, :], att4[:, i, :], AF.Identity,
                        bias=nb4[:, i:i + 1], scale=rstd4[:, i:i + 1]),
                        rd=((P + "att4", i),) + rnk4, wr=((P + "att4", i),))
                    b.v(lambda v, i=i: v.tensor_add(
                        att4[:, i, :], att4[:, i, :], xt4[:, i % 2, :]),
                        rd=((P + "att4", i), (P + "xt4", i % 2)),
                        wr=((P + "att4", i),))
                    pout.dma(xout_rows[row], att4[:, i, :],
                             rd=((P + "att4", i),))
                    if i + 2 < 4:
                        pin.dma(xt4[:, i % 2, :],
                                xin_rows[s * 16 + it * 4 + i + 2],
                                wr=((P + "xt4", i % 2), (xk4, i % 2)))

        it_pending = [None]

        b.warmup(psc[0][:, 0, :], (P + "psc", 0, 0), ident[:],
                 kT[:, 0, 0:512], P + "ident", KKT(0))
        for it in range(4):
            pendG = [[], []]
            gtail = None
            for hp in range(8):
                emit_sc_glb(hp, it, 0)
                for jt in range(16):
                    if jt < 15:
                        emit_sc_glb(hp, it, jt + 1)
                    if jt == 0 and gtail is not None:
                        gtail()
                        gtail = None
                    if hp == 0 and jt == 2 and it_pending[0] is not None:
                        it_pending[0]()
                        it_pending[0] = None
                    emit_pv_glb(hp, jt)
                gtail = (lambda hp=hp: tail_glb(hp))
            gtail()
            ltail = None
            for hp in range(8):
                for ib in range(4):
                    emit_sc_loc(hp, it, ib)
                    if ltail is not None:
                        ltail()
                        ltail = None
                    ltail = (lambda hp=hp, ib=ib: tail_loc(hp, it, ib))
            ltail()
            b.v(lambda v: v.memset(fen[:], 0.0),
                rd=((P + "po", 0), (P + "po", 1)), wr=(P + "Dfen",))
            it_pending[0] = (lambda it=it: it_tail(it))
        it_pending[0]()
        b.close_scope()

    for s in range(2):
        _seq(s)
    b.barrier_wait("sync", pout.drain())
    return pout


from concourse.bass_utils import run_bass_kernel_spmd


def _ap(nc, n, sh, dt=F32, kind="ExternalInput"):
    return nc.dram_tensor(n, sh, dt, kind=kind).ap()


def build_fused():
    nc = bass.Bass("TRN2", target_bir_lowering=False, debug=False)
    x_in = _ap(nc, "x_in", [4096, 1024])
    wk = _ap(nc, "wk", [1024, 1024])
    wv = _ap(nc, "wv", [1024, 1024])
    wr = _ap(nc, "wr", [1024, 1024])
    p1p = {n: _ap(nc, n, [128, 8]) for n in
           ("ln1w", "ln1b", "tmk", "tmv", "tmr", "td", "tf")}
    wo = _ap(nc, "wo", [1024, 1024])
    fwk = _ap(nc, "fwk", [1024, 4096])
    frc = _ap(nc, "frc", [1024, 1024])
    fwv = _ap(nc, "fwv", [4096, 1024])
    p2p = {n: _ap(nc, n, [128, 8]) for n in ("ln2w", "ln2b", "ftmk", "ftmr")}
    g_wrT = _ap(nc, "g_wrT", [1024, 1024])
    g_wkT = _ap(nc, "g_wkT", [1024, 1024])
    g_wvT = _ap(nc, "g_wvT", [1024, 1024])
    g_mur = _ap(nc, "g_mur", [128, 8])
    g_muk = _ap(nc, "g_muk", [128, 8])
    h_wrT = _ap(nc, "h_wrT", [1024, 1024])
    h_wkT = _ap(nc, "h_wkT", [1024, 1024])
    h_wvT = _ap(nc, "h_wvT", [1024, 1024])
    h_mur = _ap(nc, "h_mur", [128, 8])
    h_muk = _ap(nc, "h_muk", [128, 8])
    wqkv = _ap(nc, "wqkv", [1024, 3072])
    wout = _ap(nc, "wout", [1024, 1024])
    out = _ap(nc, "out", [4096, 1024], F32, kind="ExternalOutput")
    mT = _ap(nc, "mT", [1024, 4096], BF16, kind="Internal")
    z_scr = _ap(nc, "z_scr", [4096, 1024], F32, kind="Internal")
    actT = _ap(nc, "actT", [4096, 4096], BF16, kind="Internal")
    srf = _ap(nc, "srf", [4096, 1024], BF16, kind="Internal")
    odil = _ap(nc, "odil", [4096, 1024], BF16, kind="Internal")
    x1 = _ap(nc, "x1", [4096, 1024], F32, kind="Internal")
    x2 = _ap(nc, "x2", [4096, 1024], F32, kind="Internal")
    x3 = _ap(nc, "x3", [4096, 1024], F32, kind="Internal")

    b = Builder(nc)
    b.open_scope()
    emit_rwkv_mixer(b, x_in, mT, wk, wv, wr, p1p["ln1w"], p1p["ln1b"],
                    p1p["tmk"], p1p["tmv"], p1p["tmr"], p1p["td"], p1p["tf"],
                    pref="A")
    b.close_scope()
    b.open_scope()
    emit_rwkv_ffn(b, x_in, mT, x1, wo, fwk, frc, fwv, p2p["ln2w"],
                  p2p["ln2b"], p2p["ftmk"], p2p["ftmr"], z_scr, actT, srf,
                  pref="F")
    b.close_scope()
    b.open_scope()
    emit_channel_ffn(b, x1, x2, g_wrT, g_wkT, g_wvT, g_mur, g_muk, pref="G")
    b.close_scope()
    b.open_scope()
    emit_l1_attn(b, x2, x3, wqkv, wout, odil, pref="L")
    b.close_scope()
    b.open_scope()
    emit_channel_ffn(b, x3, out, h_wrT, h_wkT, h_wvT, h_mur, h_muk, pref="H")
    b.close_scope()
    b.finalize()
    return nc


def _pack8(v):
    return np.ascontiguousarray(np.asarray(v, np.float32).reshape(8, 128).T)


def _cc(a):
    return np.ascontiguousarray(np.asarray(a, np.float32))


def kernel(**inp):
    NCORES = 8
    x = np.asarray(inp["L0_attn_x"], np.float32)
    shards = [np.ascontiguousarray(x[2 * c:2 * c + 2].reshape(4096, 1024))
              for c in range(NCORES)]
    shared = {
        "wk": _cc(inp["L0_attn_w_key"]), "wv": _cc(inp["L0_attn_w_value"]),
        "wr": _cc(inp["L0_attn_w_recept"]),
        "ln1w": _pack8(inp["L0_attn_ln1_w"]),
        "ln1b": _pack8(inp["L0_attn_ln1_b"]),
        "tmk": _pack8(inp["L0_attn_time_mix_k"]),
        "tmv": _pack8(inp["L0_attn_time_mix_v"]),
        "tmr": _pack8(inp["L0_attn_time_mix_r"]),
        "td": _pack8(inp["L0_attn_time_decay"]),
        "tf": _pack8(inp["L0_attn_time_first"]),
        "wo": _cc(inp["L0_attn_w_output"]), "fwk": _cc(inp["L0_attn_f_w_key"]),
        "frc": _cc(inp["L0_attn_f_w_recept"]),
        "fwv": _cc(inp["L0_attn_f_w_value"]),
        "ln2w": _pack8(inp["L0_attn_ln2_w"]),
        "ln2b": _pack8(inp["L0_attn_ln2_b"]),
        "ftmk": _pack8(inp["L0_attn_f_time_mix_k"]),
        "ftmr": _pack8(inp["L0_attn_f_time_mix_r"]),
        "g_wrT": _cc(np.asarray(inp["L0_ffn_W_r"]).T),
        "g_wkT": _cc(np.asarray(inp["L0_ffn_W_k"]).T),
        "g_wvT": _cc(np.asarray(inp["L0_ffn_W_v"]).T),
        "g_mur": _pack8(inp["L0_ffn_mu_r"]),
        "g_muk": _pack8(inp["L0_ffn_mu_k"]),
        "wqkv": _cc(inp["L1_attn_w_qkv"]), "wout": _cc(inp["L1_attn_w_out"]),
        "h_wrT": _cc(np.asarray(inp["L1_ffn_W_r"]).T),
        "h_wkT": _cc(np.asarray(inp["L1_ffn_W_k"]).T),
        "h_wvT": _cc(np.asarray(inp["L1_ffn_W_v"]).T),
        "h_mur": _pack8(inp["L1_ffn_mu_r"]),
        "h_muk": _pack8(inp["L1_ffn_mu_k"]),
    }
    nc = build_fused()
    res = run_bass_kernel_spmd(
        nc, [{"x_in": s, **shared} for s in shards],
        core_ids=list(range(NCORES)))
    out = np.concatenate(
        [r["out"].reshape(2, 2048, 1024) for r in res.results], axis=0)
    return np.ascontiguousarray(out.astype(np.float32))
